# version 13
# baseline (speedup 1.0000x reference)
import contextlib
import numpy as np
import concourse.bass as bass
import concourse.mybir as mybir
from concourse.library_config import mlp as MLP_LIB
from concourse.bass_utils import run_bass_kernel_spmd

F32 = mybir.dt.float32
BF16 = mybir.dt.bfloat16
AF = mybir.ActivationFunctionType
ALU = mybir.AluOpType
AX = mybir.AxisListType

B, S, DM, H, D = 4, 4096, 1024, 16, 64
I32 = mybir.dt.int32
I16 = mybir.dt.int16
TPC, FF, NE, QT = 2048, 2048, 8, 512
NQT, NKC, NDC, NFC, NXT = TPC // QT, S // 128, DM // 128, FF // 128, S // 128
CAP = 768
DBG_SKIP_ROUTED = False
BIGF = 1.0e6

ENGS = ("t", "s", "v", "g", "y")


class Prog:

    def __init__(self):
        self.recs = {e: [] for e in ENGS}
        self.cnt = {}

    def emit(self, eng, fn, waits=(), inc=True, sem=None):
        semn = sem or eng
        inc_k = 16 if eng in ("g", "y") else 1
        ws = [(e, v) for (e, v) in waits if v and v > 0]
        self.recs[eng].append((fn, ws, semn if inc else None, inc_k))
        if inc:
            self.cnt[semn] = self.cnt.get(semn, 0) + inc_k
        return self.cnt.get(semn, 0)

    def finalize(self, nc, sems):
        def run(eng_obj, name):
            lw = {}
            for fn, ws, semn, inc_k in self.recs[name]:
                for se, val in ws:
                    if val > lw.get(se, 0):
                        eng_obj.wait_ge(sems[se], val)
                        lw[se] = val
                inst = fn(eng_obj)
                if semn is not None:
                    inst.then_inc(sems[semn], inc_k)

        with nc.Block() as block:
            block.tensor(lambda e: run(e, "t"))
            block.scalar(lambda e: run(e, "s"))
            block.vector(lambda e: run(e, "v"))
            block.gpsimd(lambda e: run(e, "g"))
            block.sync(lambda e: run(e, "y"))


def build_full():
    nc = bass.Bass("TRN2", target_bir_lowering=False,
                   num_swdge_queues=4)

    dt_ = nc.dram_tensor
    xr = dt_("xr", [S, DM], F32, kind="ExternalInput")
    jv_d = dt_("jv", [1, S], F32, kind="ExternalInput")
    eye_d = dt_("eye", [128, 128], F32, kind="ExternalInput")
    wq_d = dt_("wq", [DM, DM], F32, kind="ExternalInput")
    wk_d = dt_("wk", [DM, DM], F32, kind="ExternalInput")
    wv_d = dt_("wv", [DM, DM], F32, kind="ExternalInput")
    wo_d = dt_("wo", [DM, DM], F32, kind="ExternalInput")
    wfq_d = dt_("wfq", [DM, DM], F32, kind="ExternalInput")
    wfg_d = dt_("wfg", [DM, DM], F32, kind="ExternalInput")
    bq_d = dt_("bq_m", [128, 8], F32, kind="ExternalInput")
    bk_d = dt_("bk_m", [128, 8], F32, kind="ExternalInput")
    bo_d = dt_("bo_m", [128, 8], F32, kind="ExternalInput")
    bfq_d = dt_("bfq_m", [128, 8], F32, kind="ExternalInput")
    bfg_d = dt_("bfg_m", [128, 8], F32, kind="ExternalInput")
    bv_d = dt_("bv_m", [128, 8], F32, kind="ExternalInput")
    bvr_d = dt_("bv_r", [1, DM], F32, kind="ExternalInput")
    upz_d = dt_("upz_m", [128, 512], F32, kind="ExternalInput")
    wg_d = dt_("wg_m", [128, 64], F32, kind="ExternalInput")
    ws1_d = dt_("ws1", [DM, FF], F32, kind="ExternalInput")
    ws3_d = dt_("ws3", [DM, FF], F32, kind="ExternalInput")
    ws2_d = dt_("ws2", [FF, DM], F32, kind="ExternalInput")
    we1_d = dt_("we1", [NE, DM, FF], F32, kind="ExternalInput")
    we3_d = dt_("we3", [NE, DM, FF], F32, kind="ExternalInput")
    we2_d = dt_("we2", [NE, FF, DM], F32, kind="ExternalInput")
    out_d = dt_("out", [TPC, DM], F32, kind="ExternalOutput")
    rbd = dt_("rbd", [1, 2 * QT], F32, kind="Internal")
    y1d = dt_("y1d", [TPC, DM], BF16, kind="Internal")
    y1f = dt_("y1f", [TPC, DM], F32, kind="Internal")
    rdd = dt_("rdd", [TPC + 128, DM], F32, kind="Internal")
    mdd = dt_("mdd", [NE * CAP + 128, 64], F32, kind="Internal")
    mdc = dt_("mdc", [NE * CAP + 128, 2], F32, kind="Internal")
    mdc2 = dt_("mdc2", [NE, 2 * CAP], F32, kind="Internal")
    sld = dt_("sld", [1, 4 * TPC], F32, kind="Internal")
    t16d = dt_("t16d", [1, 128], F32, kind="Internal")
    b16d = dt_("b16d", [1, 128], F32, kind="Internal")
    ltri_d = dt_("ltri", [128, 128], F32, kind="ExternalInput")
    ecap_d = dt_("ecap", [128, 8], F32, kind="ExternalInput")

    al = nc.alloc_sbuf_tensor
    eye_f = al("eye_f", [128, 128], F32)
    eye_b = al("eye_b", [128, 128], BF16)
    ones1_b = al("ones1_b", [1, 128], BF16)
    ones128_b = al("ones128_b", [128, 1], BF16)
    cpool = al("cpool", [128, 2], F32)
    bq_sb = al("bq_sb", [128, 8], F32)
    bk_sb = al("bk_sb", [128, 8], F32)
    bo_sb = al("bo_sb", [128, 8], F32)
    bfq_sb = al("bfq_sb", [128, 8], F32)
    bfg_sb = al("bfg_sb", [128, 8], F32)
    cbias = al("cbias", [128, 8], F32)
    bv_sb = al("bv_sb", [128, 8], F32)
    wg_b = al("wg_b", [128, 64], BF16)
    xm_sb = al("xm_sb", [128, 8], F32)
    xm_b = al("xm_b", [128, 8], BF16)
    glo_sb = al("glo_sb", [128, 8], F32)
    glo_b = al("glo_b", [128, 8], BF16)
    rowsF = al("rowsF", [1, 1024], F32)
    recr_f = [rowsF[0:1, 0:512], rowsF[0:1, 512:1024]]
    rowsB = al("rowsB", [1, 2048], BF16)
    mrow_b, irow_b = rowsB[0:1, 0:512], rowsB[0:1, 512:1024]
    recr = [rowsB[0:1, 1024:1536], rowsB[0:1, 1536:2048]]
    ltri_b = al("ltri_b", [128, 128], BF16)
    ecap_sb = al("ecap_sb", [128, 8], F32)
    selm = al("selm", [128, 128], BF16)
    wfull = al("wfull", [128, 128], F32)
    posg = al("posg", [128, 128], F32)
    base_b = al("base_b", [128, 128], F32)
    t16 = al("t16", [16, 8], F32)
    t16b = al("t16b", [16, 8], BF16)
    prow = al("prow", [1, 256], F32)
    slt = al("slt", [128, 160], F32)
    sl12 = al("sl12", [128, 8], F32)
    slotf = al("slotf", [128, 32], F32)
    tokf = al("tokf", [128, 16], F32)
    pre2 = al("pre2", [128, 98], F32)
    wv1s = al("wv1s", [128, 16], F32)
    wv2s = al("wv2s", [128, 16], F32)
    mq = [al(f"mq{c}", [128, TPC], BF16) for c in range(NDC)]
    y1 = mq

    ap_ = nc.alloc_psum_tensor
    ps_s = [ap_(f"ps_s{p}", [128, 2 * QT], F32) for p in range(2)]
    ps_c = [ap_(f"ps_c{i}", [128, QT], F32) for i in range(2)]
    sc_ptr = contextlib.ExitStack()
    ps_tr = [sc_ptr.enter_context(nc.psum_tensor(f"ps_tr{p}", [128, 1024],
                                                 BF16)) for p in range(2)]

    stk = contextlib.ExitStack()
    STREAMS = ["t", "s", "v", "y_jv", "y_eye", "y_x0", "y_x1", "y_st0",
               "y_st1", "y_st2", "y_st3", "g_sm", "g_j", "g_wb", "g_upz", "g_bv", "g_kv0",
               "g_kv1", "g_c30", "g_c31", "g_wr8", "g_st0", "g_st1",
               "g_we13", "g_we2", "y_rb", "y_bv", "g_wv", "g_pf", "y_ps",
               "g_sc", "g_ix", "g_ga0", "g_ga1", "y_rd0", "y_rd1", "y_y10",
               "y_y11",
               "y_yb0", "y_yb1", "g_md", "g_mc", "g_io", "y_l2a", "y_l2b"]
    sems = {e: stk.enter_context(nc.semaphore(f"sem_{e}")) for e in STREAMS}
    P = Prog()
    nc.const_aps.register(F32, 1e-5, cpool[:, 0:1])

    def chunked(dr, c=8):
        return dr.rearrange("(c p) n -> p c n", p=128)

    def wsl(buf, di, dc):
        return buf[:, di * DM + dc * 128: di * DM + (dc + 1) * 128]

    sc_x = contextlib.ExitStack()
    x_fm = [sc_x.enter_context(nc.sbuf_tensor(f"x_fm{c}", [128, S], BF16))
            for c in range(NDC)]
    sc_ctx = contextlib.ExitStack()
    ctx_fm = [sc_ctx.enter_context(nc.sbuf_tensor(f"ctx{c}", [128, TPC], BF16))
              for c in range(NDC)]
    sc_kaug = contextlib.ExitStack()
    k_augA = sc_kaug.enter_context(nc.sbuf_tensor("k_augA", [128, S], BF16))
    k_augB = sc_kaug.enter_context(nc.sbuf_tensor("k_augB", [128, S], BF16))

    sc_p0 = contextlib.ExitStack()
    poolF = sc_p0.enter_context(nc.sbuf_tensor("poolF", [1, S], F32))
    poolB = sc_p0.enter_context(nc.sbuf_tensor("poolB", [65, S], BF16))
    stage_a = [sc_p0.enter_context(nc.sbuf_tensor(f"stg{p}", [128, DM], F32))
               for p in range(2)]
    xb_t = [sc_p0.enter_context(nc.sbuf_tensor(f"xb{p}", [128, DM], BF16))
            for p in range(2)]
    jv_sb = poolF[0:1, :]
    jv_b = poolB[0:1, :]
    jsq_b = poolB[32:33, :]
    onesrow_b = poolB[64:65, :]

    ld_jv = P.emit("y", lambda y: y.dma_start(jv_sb, jv_d[:, :]), sem="y_jv")
    ld_eye = P.emit("y", lambda y: y.dma_start(eye_f[:, :], eye_d[:, :]), sem="y_eye")
    gw_small = 0
    for dst, src in [
        (eye_b[:, :], eye_d[:, :]), (bq_sb[:, :], bq_d[:, :]),
        (bk_sb[:, :], bk_d[:, :]), (bo_sb[:, :], bo_d[:, :]),
        (bfq_sb[:, :], bfq_d[:, :]), (bfg_sb[:, :], bfg_d[:, :]),
        (bv_sb[:, :], bv_d[:, :]), (wg_b[:, :], wg_d[:, :]),
        (ltri_b[:, :], ltri_d[:, :]), (ecap_sb[:, :], ecap_d[:, :]),
    ]:
        gw_small = P.emit("g", lambda g, d=dst, s=src: g.dma_start(d, s), sem="g_sm")

    def memsets(v):
        v.memset(pre2[:, 0:98:2], -1.0)
        v.memset(pre2[:, 1:98:2], 0.0)
        v.memset(cpool[:, 0:1], 1e-5)
        v.memset(ones1_b[:, :], 1.0)
        v.memset(ones128_b[:, :], 1.0)
        return v.memset(onesrow_b, 1.0)
    v_init = P.emit("v", memsets)
    P.emit("g", lambda g: g.dma_start(
        bass.AP(mdd, 0, [[64, 128], [8192, 49], [1, 2]]),
        pre2[:, :].rearrange("p (j e) -> p j e", e=2)),
        waits=[("v", v_init)], sem="g_pf")
    P.emit("g", lambda g: g.iota(tokf[:, :], [[128, 16]], base=1,
                                 channel_multiplier=1,
                                 allow_small_or_imprecise_dtypes=True),
           sem="g_io")
    gw_io = P.cnt["g_io"]
    P.emit("g", lambda g: g.load_library(MLP_LIB), inc=False)
    gw_pf = P.cnt["g_pf"]
    v_jsq = P.emit("v", lambda v: v.tensor_tensor(jsq_b, jv_sb, jv_sb,
                                                  ALU.mult),
                   waits=[("y_jv", ld_jv)])
    v_jb = P.emit("v", lambda v: v.tensor_copy(jv_b, jv_sb),
                  waits=[("v", v_jsq)])
    gw_j = 0
    for ka in (k_augA, k_augB):
        P.emit("g", lambda g, ka=ka: g.dma_start(ka[64:65, :], jsq_b),
               waits=[("v", v_jb), ("v", v_init)], sem="g_j")
        P.emit("g", lambda g, ka=ka: g.dma_start(ka[65:66, :], jv_b),
               sem="g_j")
        gw_j = P.emit("g", lambda g, ka=ka: g.dma_start(ka[66:67, :],
                                                        onesrow_b), sem="g_j")

    x_ld, x_cv, x_tr, x_ev = {}, {}, {}, {}
    for t in range(NXT):
        par = t % 2
        x_ld[t] = P.emit("y", lambda y, t=t, par=par: y.dma_start(
            stage_a[par][:, :], xr[t * 128:(t + 1) * 128, :]),
            waits=[("v", x_cv.get(t - 2, 0))], sem=f"y_x{par}")
        x_cv[t] = P.emit("v", lambda v, par=par: v.tensor_copy(
            xb_t[par][:, :], stage_a[par][:, :]),
            waits=[(f"y_x{par}", x_ld[t]), ("t", x_tr.get(t - 2, 0)),
                   ("v", v_init)])

        def trf(t_, par=par):
            for dc in range(NDC):
                ii = t_.transpose(ps_tr[par][:, dc * 128:(dc + 1) * 128],
                                  xb_t[par][:, dc * 128:(dc + 1) * 128],
                                  eye_b[:, :])
            return ii
        x_tr[t] = P.emit("t", trf, waits=[("v", x_cv[t]),
                                          ("g_sm", gw_small),
                                          ("v", x_ev.get(t - 2, 0))])

        def evf(v, t=t, par=par):
            for dc in range(NDC):
                ii = v.tensor_copy(x_fm[dc][:, t * 128:(t + 1) * 128],
                                   ps_tr[par][:, dc * 128:(dc + 1) * 128])
            return ii
        x_ev[t] = P.emit("v", evf, waits=[("t", x_tr[t])])
    x_done = x_ev[NXT - 1]
    xtr_last = x_tr[NXT - 1]

    sc_p0.close()
    sc_ptr.close()
    ps_d = nc.alloc_psum_tensor("ps_d", [128, 2 * QT], F32)

    sc_c3 = contextlib.ExitStack()
    C3 = sc_c3.enter_context(nc.sbuf_tensor("C3", [80, TPC], BF16))
    sc_w = contextlib.ExitStack()
    w_big1 = sc_w.enter_context(nc.sbuf_tensor("w_big1", [128, 8 * DM], BF16))
    w_big2 = w_big1

    gw_wq = P.emit("g", lambda g: g.dma_start(
        w_big1[:, :].rearrange("p (c n) -> p c n", c=8), chunked(wq_d[:, :])),
        waits=[("t", xtr_last), ("v", x_done), ("g_j", gw_j)], sem="g_wb")

    mq_mm, mq_ev = {}, {}
    for qc in range(NQT):
        for dc in range(NDC):
            i = qc * NDC + dc
            par = i % 2

            def mmf(t_, qc=qc, dc=dc, par=par):
                for di in range(NDC):
                    ii = t_.matmul(ps_s[par][:, 0:QT], wsl(w_big1, di, dc),
                                   x_fm[di][:, qc * QT:(qc + 1) * QT],
                                   start=(di == 0), stop=(di == NDC - 1))
                return ii
            mq_mm[i] = P.emit("t", mmf, waits=[("v", x_done), ("g_wb", gw_wq),
                                               ("v", mq_ev.get(i - 2, 0))])
            mq_ev[i] = P.emit("v", lambda v, qc=qc, dc=dc, par=par:
                              v.tensor_scalar(mq[dc][:, qc * QT:(qc + 1) * QT],
                                              ps_s[par][:, 0:QT],
                                              bq_sb[:, dc:dc + 1], None,
                                              ALU.add),
                              waits=[("t", mq_mm[i])])
    mq_done = mq_ev[NQT * NDC - 1]

    def xmeanf(v):
        for dc in range(NDC):
            ii = v.reduce_sum(xm_sb[:, dc:dc + 1], x_fm[dc][:, :], axis=AX.X)
        return ii
    v_xm = P.emit("v", xmeanf, waits=[("v", x_done)])
    v_xmb = P.emit("v", lambda v: v.tensor_scalar(
        xm_b[:, :], xm_sb[:, :], 1.0 / S, None, ALU.mult), waits=[("v", v_xm)])

    def glof(t_):
        for dc in range(NDC):
            for di in range(NDC):
                ii = t_.matmul(ps_c[0][:, dc:dc + 1], wsl(w_big1, di, dc),
                               xm_b[:, di:di + 1], start=(di == 0),
                               stop=(di == NDC - 1))
        return ii
    t_glo = P.emit("t", glof, waits=[("v", v_xmb), ("g_wb", gw_wq)])
    v_glo = P.emit("v", lambda v: v.tensor_tensor(
        glo_sb[:, :], ps_c[0][:, 0:8], bq_sb[:, :], ALU.add),
        waits=[("t", t_glo)])
    v_glob = P.emit("v", lambda v: v.tensor_copy(glo_b[:, :], glo_sb[:, :]),
                    waits=[("v", v_glo)])

    def gff(t_):
        for dc in range(NDC):
            for di in range(NDC):
                ii = t_.matmul(ps_c[1][:, dc:dc + 1], wsl(w_big2, di, dc),
                               glo_b[:, di:di + 1], start=(di == 0),
                               stop=(di == NDC - 1))
        return ii
    gw_wfg = P.emit("g", lambda g: g.dma_start(
        w_big1[:, :].rearrange("p (c n) -> p c n", c=8), chunked(wfg_d[:, :])),
        waits=[("t", t_glo), ("t", mq_mm[NQT * NDC - 1])], sem="g_wb")
    t_gf = P.emit("t", gff, waits=[("v", v_glob), ("g_wb", gw_wfg)])
    v_cb1 = P.emit("v", lambda v: v.tensor_tensor(
        cbias[:, :], ps_c[1][:, 0:8], bfq_sb[:, :], ALU.add),
        waits=[("t", t_gf)])
    v_cb = P.emit("v", lambda v: v.tensor_tensor(
        cbias[:, :], cbias[:, :], bfg_sb[:, :], ALU.add), waits=[("v", v_cb1)])

    gw_wfq = P.emit("g", lambda g: g.dma_start(
        w_big2[:, :].rearrange("p (c n) -> p c n", c=8), chunked(wfq_d[:, :])),
        waits=[("t", t_gf)], sem="g_wb")

    sc_p2 = contextlib.ExitStack()
    upz_b = sc_p2.enter_context(nc.sbuf_tensor("upz_b", [128, 512], BF16))
    gw_upz = P.emit("g", lambda g: g.dma_start(upz_b[:, :], upz_d[:, :]),
                    waits=[("t", xtr_last), ("v", x_done), ("g_j", gw_j)],
                    sem="g_upz")
    spz = sc_p2.enter_context(nc.sbuf_tensor("spz", [128, 4096], F32))
    Sp, Sz = spz[0:16, 0:TPC], spz[32:48, 0:TPC]
    cA, cB = spz[64:80, 0:TPC], spz[64:80, TPC:2 * TPC]
    ctp = sc_p2.enter_context(nc.sbuf_tensor("ctp", [128, 1024], BF16))
    c_t_sb = [ctp[:, 0:QT], ctp[:, QT:2 * QT]]

    c_mm, c_tanh, pz_mm, sig = {}, {}, {}, {}
    for qc in range(NQT):
        for dc in range(NDC):
            i = qc * NDC + dc
            par = i % 2

            def cmf(t_, qc=qc, dc=dc, par=par):
                for di in range(NDC):
                    ii = t_.matmul(ps_s[par][:, 0:QT], wsl(w_big2, di, dc),
                                   mq[di][:, qc * QT:(qc + 1) * QT],
                                   start=(di == 0), stop=(di == NDC - 1))
                return ii
            c_mm[i] = P.emit("t", cmf,
                             waits=[("v", v_cb), ("g_wb", gw_wfq), ("v", mq_done),
                                    ("s", c_tanh.get(i - 2, 0))])
            c_tanh[i] = P.emit("s", lambda s, dc=dc, par=par: s.activation(
                c_t_sb[par], ps_s[par][:, 0:QT], AF.Tanh,
                bias=cbias[:, dc:dc + 1]), waits=[("t", c_mm[i])])
            pz_mm[i] = P.emit("t", lambda t_, dc=dc, par=par: t_.matmul(
                ps_c[0][0:64, 0:QT], upz_b[:, dc * 64:(dc + 1) * 64],
                c_t_sb[par], start=(dc == 0), stop=(dc == NDC - 1)),
                waits=[("s", c_tanh[i]), ("s", sig.get(qc - 1, 0)),
                       ("v", v_glo if i == 0 else 0),
                       ("g_upz", gw_upz if i == 0 else 0)])
        sig[qc] = P.emit("s", lambda s, qc=qc: (
            s.activation(Sp[:, qc * QT:(qc + 1) * QT], ps_c[0][0:16, 0:QT],
                         AF.Sigmoid),
            s.activation(Sz[:, qc * QT:(qc + 1) * QT], ps_c[0][32:48, 0:QT],
                         AF.Sigmoid))[-1],
            waits=[("t", pz_mm[qc * NDC + NDC - 1])])
    sig_done = sig[NQT - 1]
    pz_last = pz_mm[NQT * NDC - 1]

    steps = [
        lambda v: v.reciprocal(cA, Sz),
        lambda v: v.tensor_tensor(cA, cA, cA, ALU.mult),
        lambda v: v.tensor_scalar(cA, cA, -1.1920928955078125e-07,
                                  None, ALU.mult),
        lambda v: v.tensor_scalar(cB, Sp, float(S), None, ALU.mult),
        lambda v: v.tensor_copy(C3[0:16, 0:TPC], cA),
        lambda v: v.tensor_tensor(cA, cA, cB, ALU.mult),
        lambda v: v.tensor_scalar(C3[32:48, 0:TPC], cA, -2.0, None,
                                  ALU.mult),
        lambda v: v.tensor_tensor(cA, cA, cB, ALU.mult),
        lambda v: v.tensor_scalar(C3[64:80, 0:TPC], cA, -4.0, None,
                                  ALU.add),
    ]
    vprev = P.emit("v", steps[0], waits=[("s", sig_done)])
    for f in steps[1:]:
        vprev = P.emit("v", f, waits=[("v", vprev)])
    coeff_done = vprev

    sc_p2.close()
    sc_w.close()

    sc_att2 = contextlib.ExitStack()
    v_augs = [sc_att2.enter_context(nc.sbuf_tensor(f"v_aug{i}",
                                                   [128, NKC * 130], BF16))
              for i in range(2)]
    kv_w = [sc_att2.enter_context(nc.sbuf_tensor(f"kv{p}", [128, 1024], BF16))
            for p in range(2)]
    wv2 = sc_att2.enter_context(nc.sbuf_tensor("wv2", [128, 2048], BF16))
    DEP = 6
    e_t = [sc_att2.enter_context(nc.sbuf_tensor(f"e_t{p}", [128, 2 * QT],
                                                BF16)) for p in range(DEP)]
    rhs_aug = [sc_att2.enter_context(nc.sbuf_tensor(f"rhs{p}", [128, QT],
                                                    BF16)) for p in range(4)]
    rbp = sc_att2.enter_context(nc.sbuf_tensor("rbp", [128, QT], F32))
    rb_sb = [rbp[0:64, :], rbp[64:128, :]]
    bvb2 = sc_att2.enter_context(nc.sbuf_tensor("bvb2", [128, 256], F32))

    def vones(v):
        for va_t in v_augs:
            va = va_t[:, :].rearrange("p (t c) -> p t c", c=130)
            v.memset(va[:, :, 64:65], 1.0)
            ii = v.memset(va[:, :, 129:130], 1.0)
        return ii
    v_vones = P.emit("v", vones, waits=[("t", pz_last), ("v", coeff_done)])

    SB3 = [ps_s[0], ps_s[1], ps_d]

    AT = {}
    exp_hist = []
    ctx_hist = []
    pair_t_end = {}
    for p in range(NE):
        wpar = p % 2
        cols = slice(p * 128, (p + 1) * 128)
        kv_gw = P.emit("g", lambda g, wpar=wpar, cols=cols: g.dma_start(
            kv_w[wpar][:, :].rearrange("p (c n) -> p c n", c=8),
            chunked(wk_d[:, cols])),
            waits=[("t", pair_t_end.get(p - 2, pz_last)), ("g_j", gw_j),
                   ("v", coeff_done if p == 0 else 0),
                   ("s", sig_done if p == 0 else 0)], sem=f"g_kv{wpar}")
        if p % 2 == 0:
            cols2 = slice(p * 128, (p + 2) * 128)
            gw_wv = P.emit("g", lambda g, cols2=cols2: g.dma_start(
                wv2[:, :].rearrange("p (c n) -> p c n", c=8),
                chunked(wv_d[:, cols2])),
                waits=[("t", AT.get((p - 2, "v_done"), pz_last))], sem="g_wv")
            bv_dma = P.emit("y", lambda y, p=p: y.dma_start(
                bvb2[:, :], bass.AP(bvr_d, p * 128, [[0, 128], [1, 256]])),
                waits=[("v", AT.get((p - 2, "bv_used"), 0)),
                       ("v", v_init)], sem="y_bv")

        kev = {}
        for kt in range(8):
            par = kt % 2

            def kmf(t_, kt=kt, par=par, wpar=wpar):
                for di in range(NDC):
                    ii = t_.matmul(ps_s[par][:, 0:QT],
                                   kv_w[wpar][:, di * 128:(di + 1) * 128],
                                   x_fm[di][:, kt * QT:(kt + 1) * QT],
                                   start=(di == 0), stop=(di == NDC - 1))
                return ii
            w = [(f"g_kv{wpar}", kv_gw), ("v", kev.get(kt - 2, 0))]
            if kt == 0:
                w += [("s", exp_hist[-1] if exp_hist else sig_done),
                      ("v", AT.get((p - 1, NQT - 1, "ctxe"),
                                   max(coeff_done, mq_done)))]
            tk = P.emit("t", kmf, waits=w)
            kev[kt] = P.emit("v", lambda v, kt=kt, par=par, p=p: (
                v.tensor_scalar(k_augA[0:64, kt * QT:(kt + 1) * QT],
                                ps_s[par][0:64, 0:QT], bk_sb[0:64, p:p + 1],
                                0.125, ALU.add, ALU.mult),
                v.tensor_scalar(k_augB[0:64, kt * QT:(kt + 1) * QT],
                                ps_s[par][64:128, 0:QT], bk_sb[64:128, p:p + 1],
                                0.125, ALU.add, ALU.mult))[-1],
                waits=[("t", tk),
                       ("t", AT.get((p - 1, NQT - 1, "last_sc"), 0)
                        if kt == 0 else 0)])

        if p % 2 == 0:
            vev = {}
            for tt in range(NKC):
                par = tt % 2

                def vmf(t_, tt=tt, par=par):
                    for di in range(NDC):
                        ii = t_.matmul(ps_s[par][:, QT:QT + 256],
                                       x_fm[di][:, tt * 128:(tt + 1) * 128],
                                       wv2[:, di * 256:(di + 1) * 256],
                                       start=(di == 0), stop=(di == NDC - 1))
                    return ii
                w = [("v", vev.get(tt - 2, 0)), ("g_wv", gw_wv)]
                if tt == 0:
                    w += [("t", AT.get((p - 1, NQT - 1, NKC - 1, "ctx"), 0))]
                tv = P.emit("t", vmf, waits=w)
                base = tt * 130

                def vevf(v, base=base, par=par):
                    v.tensor_tensor(v_augs[0][:, base:base + 64],
                                    ps_s[par][:, QT:QT + 64], bvb2[:, 0:64],
                                    ALU.add)
                    v.tensor_tensor(v_augs[0][:, base + 65:base + 129],
                                    ps_s[par][:, QT + 64:QT + 128],
                                    bvb2[:, 64:128], ALU.add)
                    v.tensor_tensor(v_augs[1][:, base:base + 64],
                                    ps_s[par][:, QT + 128:QT + 192],
                                    bvb2[:, 128:192], ALU.add)
                    return v.tensor_tensor(v_augs[1][:, base + 65:base + 129],
                                           ps_s[par][:, QT + 192:QT + 256],
                                           bvb2[:, 192:256], ALU.add)
                vev[tt] = P.emit("v", vevf,
                                 waits=[("t", tv), ("y_bv", bv_dma),
                                        ("v", v_vones if tt == 0 and p == 0
                                         else 0)])
            AT[(p, "v_done")] = P.cnt["t"]
            AT[(p, "bv_used")] = vev[NKC - 1]
            kv_done_v = max(vev[NKC - 1], kev[7])
        else:
            kv_done_v = kev[7]
        v_aug = v_augs[p % 2]

        def emit_rhs(pp, qc):
            for hi in range(2):
                bi = 2 * (qc % 2) + hi
                h = 2 * pp + hi
                guard = AT.get((pp, qc - 2, "last_sc"),
                               AT.get((pp - 1, NQT - 1, "last_sc"), pz_last))
                AT[(pp, qc, hi, "q")] = P.emit(
                    "v", lambda v, bi=bi, hi=hi, pp=pp, qc=qc: v.tensor_copy(
                        rhs_aug[bi][0:64, :],
                        mq[pp][hi * 64:(hi + 1) * 64, qc * QT:(qc + 1) * QT]),
                    waits=[("t", guard)])
                for ri in range(3):
                    AT[(pp, qc, hi, "c3")] = P.emit(
                        "g", lambda g, bi=bi, h=h, qc=qc, ri=ri: g.dma_start(
                            rhs_aug[bi][64 + ri:65 + ri, 0:QT],
                            C3[h + 32 * ri:h + 32 * ri + 1,
                               qc * QT:(qc + 1) * QT]),
                        waits=[("v", AT[(pp, qc, hi, "q")])]
                        if ri == 0 else [], sem=f"g_c3{qc % 2}")

        if p == 0:
            emit_rhs(p, 0)
        emit_rhs(p, 1)
        for qc in range(NQT):
            def emit_ctx(kc):
                g = len(ctx_hist)
                assert g % NKC == kc or True
                gb = (qc * NKC + kc)

                def ctf(t_, kc=kc, va=v_aug, eb=e_t[len(ctx_hist) % DEP]):
                    base = kc * 130
                    t_.matmul(ps_c[0][0:65, :], va[:, base:base + 65],
                              eb[:, 0:QT], start=(kc == 0),
                              stop=(kc == NKC - 1))
                    return t_.matmul(ps_c[1][0:65, :],
                                     va[:, base + 65:base + 130],
                                     eb[:, QT:2 * QT],
                                     start=(kc == 0), stop=(kc == NKC - 1))
                wct = [("s", AT[(p, qc, kc, "exp")])]
                if kc == 0:
                    wct += [("v", AT.get((p, qc - 1, "ctxe"),
                                         AT.get((p - 1, NQT - 1, "ctxe"), 0))),
                            ("v", kv_done_v)]
                AT[(p, qc, kc, "ctx")] = P.emit("t", ctf, waits=wct)
                ctx_hist.append(AT[(p, qc, kc, "ctx")])

            for kc in range(NKC):
                if kc >= DEP:
                    emit_ctx(kc - DEP)
                g = len(exp_hist)
                b3 = SB3[g % 3]
                wsc = [("s", exp_hist[g - 3] if g >= 3 else 0),
                       (f"g_c3{qc % 2}", AT[(p, qc, 1, "c3")]),
                       ("v", kv_done_v)]
                if kc == 0:
                    wsc += [("g_j", gw_j if p == 0 and qc == 0 else 0)]

                def scf(t_, kc=kc, b3=b3, biA=2 * (qc % 2),
                        biB=2 * (qc % 2) + 1):
                    t_.matmul(b3[:, 0:QT],
                              k_augA[0:67, kc * 128:(kc + 1) * 128],
                              rhs_aug[biA][0:67, :], start=True, stop=True)
                    return t_.matmul(b3[:, QT:2 * QT],
                                     k_augB[0:67, kc * 128:(kc + 1) * 128],
                                     rhs_aug[biB][0:67, :], start=True,
                                     stop=True)
                AT[(p, qc, kc, "sc")] = P.emit("t", scf, waits=wsc)
                AT[(p, qc, kc, "exp")] = P.emit(
                    "s", lambda s, b3=b3, eb=e_t[g % DEP]: s.activation(
                        eb[:, :], b3[:, :], AF.Exp),
                    waits=[("t", AT[(p, qc, kc, "sc")]),
                           ("t", ctx_hist[g - DEP] if g >= DEP else 0)])
                exp_hist.append(AT[(p, qc, kc, "exp")])
                if kc == 2 and qc >= 1:
                    if qc + 1 < NQT:
                        emit_rhs(p, qc + 1)
                    elif p + 1 < NE:
                        emit_rhs(p + 1, 0)
            for kc2 in range(NKC - DEP, NKC):
                emit_ctx(kc2)
            AT[(p, qc, "last_sc")] = AT[(p, qc, NKC - 1, "sc")]
            AT[(p, qc, "rec")] = P.emit("v", lambda v: (
                v.tensor_copy(recr_f[0], ps_c[0][64:65, :]),
                v.tensor_copy(recr_f[1], ps_c[1][64:65, :]))[-1],
                waits=[("t", AT[(p, qc, NKC - 1, "ctx")]),
                       ("y_rb", AT.get((p, qc - 1, "rb"),
                                       AT.get((p - 1, NQT - 1, "rb"), 0)))])
            wr_rbd = P.emit("y", lambda y: y.dma_start(
                rbd[:, :], rowsF[0:1, 0:2 * QT]),
                waits=[("v", AT[(p, qc, "rec")])], sem="y_rb")
            AT[(p, qc, "rb")] = P.emit("y", lambda y: y.dma_start(
                rbp[:, :], bass.AP(rbd, 0, [[512, 2], [0, 64], [1, QT]])),
                waits=[("y_rb", wr_rbd),
                       ("v", AT.get((p, qc - 1, "ctxe"),
                                    AT.get((p - 1, NQT - 1, "ctxe"), 0)))],
                sem="y_rb")
            AT[(p, qc, "rcp")] = P.emit("v", lambda v: (
                v.reciprocal(rb_sb[0], rb_sb[0]),
                v.reciprocal(rb_sb[1], rb_sb[1]))[-1],
                waits=[("y_rb", AT[(p, qc, "rb")])])
            AT[(p, qc, "ctxe")] = P.emit("v", lambda v, p=p, qc=qc: (
                v.tensor_tensor(ctx_fm[p][0:64, qc * QT:(qc + 1) * QT],
                                ps_c[0][0:64, :], rb_sb[0], ALU.mult),
                v.tensor_tensor(ctx_fm[p][64:128, qc * QT:(qc + 1) * QT],
                                ps_c[1][0:64, :], rb_sb[1], ALU.mult))[-1],
                waits=[("v", AT[(p, qc, "rcp")])])
        pair_t_end[p] = P.cnt["t"]
    att_v = AT[(NE - 1, NQT - 1, "ctxe")]
    att_rbs = AT[(NE - 1, NQT - 1, NKC - 1, "exp")]
    att_t = P.cnt["t"]

    sc_att2.close()
    sc_c3.close()
    sc_kaug.close()
    sc_p4 = contextlib.ExitStack()
    wo_b = sc_p4.enter_context(nc.sbuf_tensor("wo_b", [128, 8 * DM], BF16))
    zbuf_a = sc_p4.enter_context(nc.sbuf_tensor("zbuf_a", [128, 8 * QT], F32))
    mb_a = sc_p4.enter_context(nc.sbuf_tensor("mb_a", [128, QT], F32))
    ivb_a = sc_p4.enter_context(nc.sbuf_tensor("ivb_a", [128, QT], F32))
    rows_a = sc_p4.enter_context(nc.sbuf_tensor("rows_a", [1, 2048], F32))
    zbp_a = [sc_p4.enter_context(nc.sbuf_tensor(f"zbp_a{p}", [128, QT], BF16))
             for p in range(2)]
    zsp_a = [sc_p4.enter_context(nc.sbuf_tensor(f"zsp_a{p}", [128, QT], BF16))
             for p in range(2)]
    gsig = sc_p4.enter_context(nc.sbuf_tensor("gsig", [128, 32], F32))
    gt = sc_p4.enter_context(nc.sbuf_tensor("gt", [128, 160], F32))
    wtok = sc_p4.enter_context(nc.sbuf_tensor("wtok", [128, 32], F32))
    ystg = [sc_p4.enter_context(nc.sbuf_tensor(f"ystg{p}", [128, DM], F32))
            for p in range(2)]
    ystg_b = [sc_p4.enter_context(nc.sbuf_tensor(f"ystb{p}", [128, DM], BF16))
              for p in range(2)]

    gw_wo = P.emit("g", lambda g: g.dma_start(
        wo_b[:, :].rearrange("p (c n) -> p c n", c=8), chunked(wo_d[:, :])),
        waits=[("t", att_t), ("s", att_rbs), ("v", att_v)], sem="g_wb")

    def ln_phase(qc, mode, y_dst, st_, zbuf, mb_sb, invb_sb, rows4,
                 zb_t, zsq_t):
        L = {}
        mrow, srow = rows4[0:1, 0:512], rows4[0:1, 512:1024]
        vrow, vrow2 = rows4[0:1, 1024:1536], rows4[0:1, 1536:2048]
        zev = st_.setdefault("zev", {})
        for dc in range(NDC):
            gi = qc * NDC + dc
            par = gi % 2
            if mode == "attn":
                wmm = [("v", att_v), ("g_wb", gw_wo),
                       ("v", zev.get(gi - 2, 0)),
                       ("s", att_rbs if gi < 2 else 0),
                       ("s", st_.get("prev_bcs", 0) if dc < 2 else 0)]

                def mmf(t_, qc=qc, dc=dc, par=par):
                    for di in range(NDC):
                        ii = t_.matmul(ps_s[par][:, 0:QT], wsl(wo_b, di, dc),
                                       ctx_fm[di][:, qc * QT:(qc + 1) * QT],
                                       start=(di == 0), stop=(di == NDC - 1))
                    return ii
                L[(dc, "mm")] = P.emit("t", mmf, waits=wmm)
                zev[gi] = L[(dc, "z")] = P.emit(
                    "v", lambda v, qc=qc, dc=dc, par=par:
                    v.scalar_tensor_tensor(
                        zbuf[:, dc * QT:(dc + 1) * QT], ps_s[par][:, 0:QT],
                        bo_sb[:, dc:dc + 1],
                        x_fm[dc][:, qc * QT:(qc + 1) * QT], ALU.add, ALU.add),
                    waits=[("t", L[(dc, "mm")]),
                           ("t", st_.get("prev_ttr", 0) if dc < 2 else 0)])
            else:
                L[(dc, "z")] = P.emit(
                    "v", lambda v, qc=qc, dc=dc: v.scalar_tensor_tensor(
                        zbuf[:, dc * QT:(dc + 1) * QT],
                        st_["routed"][dc][:, qc * QT:(qc + 1) * QT], 1.0 / 3.0,
                        y1[dc][:, qc * QT:(qc + 1) * QT], ALU.mult, ALU.add),
                    waits=[("v", st_.get("src_v", 0)),
                           ("t", st_.get("prev_ttr", 0))])
            zg = (L.get((dc - 2, "sums"), st_.get("prev_sums_last", 0)
                        if dc < 2 else 0))
            L[(dc, "zb")] = P.emit("v", lambda v, dc=dc: v.tensor_copy(
                zb_t[dc % 2][:, :], zbuf[:, dc * QT:(dc + 1) * QT]),
                waits=[("v", L[(dc, "z")]), ("t", zg)])
            L[(dc, "zsq")] = P.emit("s", lambda s, dc=dc: s.activation(
                zsq_t[dc % 2][:, :], zbuf[:, dc * QT:(dc + 1) * QT],
                AF.Square), waits=[("v", L[(dc, "z")]), ("t", zg)])

            def emit_sums(d2):
                wsum = [("v", L[(d2, "zb")]), ("s", L[(d2, "zsq")])]
                if d2 == 0:
                    wsum += [("s", st_.get("psc_guard_s", 0)),
                             ("v", st_.get("psc_guard_v", 0))]
                L[(d2, "sums")] = P.emit("t", lambda t_, d2=d2: (
                    t_.matmul(ps_c[0][0:1, 0:QT], ones128_b[:, :],
                              zb_t[d2 % 2][:, :], start=(d2 == 0),
                              stop=(d2 == NDC - 1)),
                    t_.matmul(ps_c[1][0:1, 0:QT], ones128_b[:, :],
                              zsq_t[d2 % 2][:, :], start=(d2 == 0),
                              stop=(d2 == NDC - 1)))[-1], waits=wsum)

            if dc >= 1:
                emit_sums(dc - 1)
            if dc == NDC - 1:
                emit_sums(dc)
        r0 = P.emit("v", lambda v: v.tensor_scalar(
            mrow, ps_c[0][0:1, 0:QT], 1.0 / DM, None, ALU.mult),
            waits=[("t", L[(NDC - 1, "sums")])])
        r1 = P.emit("v", lambda v: v.tensor_scalar(
            vrow, ps_c[1][0:1, 0:QT], 1.0 / DM, None, ALU.mult),
            waits=[("v", r0)])
        r2 = P.emit("v", lambda v: v.tensor_tensor(vrow2, mrow, mrow,
                                                   ALU.mult), waits=[("v", r1)])
        r3 = P.emit("v", lambda v: v.tensor_tensor(vrow, vrow, vrow2,
                                                   ALU.subtract),
                    waits=[("v", r2)])
        r4 = P.emit("s", lambda s: s.activation(srow, vrow, AF.Sqrt,
                                                bias=1e-5), waits=[("v", r3)])
        r5 = P.emit("v", lambda v: v.reciprocal(vrow2, srow),
                    waits=[("s", r4)])
        r6 = P.emit("v", lambda v: (v.tensor_copy(mrow_b, mrow),
                                    v.tensor_copy(irow_b, vrow2))[-1],
                    waits=[("v", r5)])
        wbc = [("v", r6), ("v", L[(NDC - 1, "z")]),
               ("s", st_.get("pss_guard_s", 0)),
               ("v", st_.get("pss_guard_v", 0))]
        bct = P.emit("t", lambda t_: (
            t_.matmul(ps_s[0][:, 0:QT], ones1_b[:, :], mrow_b, start=True,
                      stop=True),
            t_.matmul(ps_s[1][:, 0:QT], ones1_b[:, :], irow_b, start=True,
                      stop=True))[-1], waits=wbc)
        bcs = P.emit("s", lambda s: (s.copy(mb_sb[:, :], ps_s[0][:, 0:QT]),
                                     s.copy(invb_sb[:, :],
                                            ps_s[1][:, 0:QT]))[-1],
                     waits=[("t", bct)])
        subv = P.emit("v", lambda v: [v.tensor_tensor(
            zbuf[:, dc * QT:(dc + 1) * QT], zbuf[:, dc * QT:(dc + 1) * QT],
            mb_sb[:, :], ALU.subtract) for dc in range(NDC)][-1],
            waits=[("s", bcs)])
        finz = P.emit("v", lambda v: [v.tensor_tensor(
            zbuf[:, dc * QT:(dc + 1) * QT], zbuf[:, dc * QT:(dc + 1) * QT],
            invb_sb[:, :], ALU.mult) for dc in range(NDC)][-1],
            waits=[("v", subv)])
        if y_dst is not None:
            fin = P.emit("v", lambda v, qc=qc: [v.tensor_copy(
                y_dst[dc][:, qc * QT:(qc + 1) * QT],
                zbuf[:, dc * QT:(dc + 1) * QT]) for dc in range(NDC)][-1],
                waits=[("v", finz)])
        else:
            fin = finz
        L["finz"] = finz
        st_["prev_sums_last"] = L[(NDC - 1, "sums")]
        st_["prev_bcs"] = bcs
        st_["pss_guard_s"] = bcs
        st_["psc_guard_v"] = r1
        L["fin"], L["bcs"], L["r1"] = fin, bcs, r1
        return L

    st1 = {"psc_guard_s": sig_done}
    gate_ws = {}
    YS = {}
    ln1_zlast = 0
    ln1_mmlast = 0
    for qc in range(NQT):
        L1 = ln_phase(qc, "attn", y1, st1, zbuf_a, mb_a, ivb_a, rows_a,
                      zbp_a, zsp_a)
        ln1_zlast = L1[(NDC - 1, "z")]
        ln1_mmlast = L1[(NDC - 1, "mm")]
        gmt = {}
        gst = {}
        for tt in range(4):
            tok = qc * QT + tt * 128

            def gmf(t_, qc=qc, tt=tt, tok=tok):
                for di in range(NDC):
                    ii = t_.matmul(ps_c[0][:, 8 * tt:8 * tt + 8],
                                   y1[di][:, tok:tok + 128],
                                   wg_b[:, di * 8:(di + 1) * 8],
                                   start=(di == 0), stop=(di == NDC - 1))
                return ii
            gmt[tt] = P.emit("t", gmf, waits=[("v", L1["fin"]),
                                              ("s", gate_ws.get((qc - 1, 3),
                                                                0))])
        for tt in range(4):
            gst[tt] = P.emit("s", lambda s, tt=tt: s.activation(
                gsig[:, 8 * tt:8 * tt + 8], ps_c[0][:, 8 * tt:8 * tt + 8],
                AF.Sigmoid), waits=[("t", gmt[tt])])
        st1["psc_guard_s"] = gst[3]

        def gchain(tt):
            gg = gsig[:, 8 * tt:8 * tt + 8]
            G = lambda a, b: gt[:, 40 * tt + a:40 * tt + b]
            wt = wtok[:, 8 * tt:8 * tt + 8]
            return [
                lambda v: v.reduce_max(G(0, 1), gg, axis=AX.X),
                lambda v: v.tensor_scalar(G(8, 16), gg, G(0, 1), None,
                                          ALU.is_equal),
                lambda v: v.scalar_tensor_tensor(G(16, 24), G(8, 16), -1e30,
                                                 gg, ALU.mult, ALU.add),
                lambda v: v.reduce_max(G(1, 2), G(16, 24), axis=AX.X),
                lambda v: v.tensor_scalar(G(24, 32), G(16, 24), G(1, 2), None,
                                          ALU.is_equal),
                lambda v: v.tensor_tensor(G(24, 32), G(24, 32), G(8, 16),
                                          ALU.add),
                lambda v: v.tensor_tensor(G(2, 3), G(0, 1), G(1, 2), ALU.add),
                lambda v: v.reciprocal(G(3, 4), G(2, 3)),
                lambda v: v.tensor_tensor(G(32, 40), gg, G(24, 32), ALU.mult),
                lambda v: v.tensor_scalar(wt, G(32, 40), G(3, 4), None,
                                          ALU.mult),
            ]
        chains = {tt: gchain(tt) for tt in range(4)}
        last_v = {tt: 0 for tt in range(4)}
        for step in range(10):
            for tt in range(4):
                w = [("s", gst[tt])] if step == 0 else [("v", last_v[tt])]
                last_v[tt] = P.emit("v", chains[tt][step], waits=w)
        for tt in range(4):
            k = qc * 4 + tt
            gate_ws[(qc, tt)] = P.emit("v", lambda v, tt=tt, k=k: (
                v.tensor_copy(selm[:, 8 * k:8 * k + 8], gt[:, 40 * tt + 24:
                                                           40 * tt + 32]),
                v.tensor_copy(wfull[:, 8 * k:8 * k + 8],
                              wtok[:, 8 * tt:8 * tt + 8]))[-1],
                waits=[("v", last_v[tt])])
        for tt in range(4):
            tok = qc * QT + tt * 128
            par = tt % 2

            def ytr(t_, tt=tt):
                for dc in range(NDC):
                    ii = t_.transpose(ps_d[:, dc * 128:(dc + 1) * 128],
                                      zbuf_a[:, dc * QT + tt * 128:
                                             dc * QT + (tt + 1) * 128],
                                      eye_f[:, :])
                return ii
            yt_t = P.emit("t", ytr, waits=[("v", L1["finz"]),
                                           ("v", YS.get(("cv", qc, tt - 1), 0)),
                                           ("y_eye", ld_eye)])
            YS[("tr", qc, tt)] = yt_t
            stg = YS.get(("st", qc, tt - 2), YS.get(("st", qc - 1, tt + 2), 0))
            stbg = YS.get(("stb", qc, tt - 2),
                          YS.get(("stb", qc - 1, tt + 2), 0))
            YS[("ev", qc, tt)] = P.emit(
                "v", lambda v, par=par: v.tensor_copy(ystg[par][:, :],
                                                      ps_d[:, :]),
                waits=[("t", yt_t), (f"y_y1{par}", stg),
                       (f"y_yb{par}", stbg)])
            YS[("cv", qc, tt)] = P.emit(
                "v", lambda v, par=par: v.tensor_copy(ystg_b[par][:, :],
                                                      ystg[par][:, :]),
                waits=[("v", YS[("ev", qc, tt)])])
            YS[("st", qc, tt)] = P.emit(
                "y", lambda y, par=par, tok=tok: y.dma_start(
                    y1f[tok:tok + 128, :], ystg[par][:, :]),
                waits=[("v", YS[("ev", qc, tt)])], sem=f"y_y1{par}")
            YS[("stb", qc, tt)] = P.emit(
                "y", lambda y, par=par, tok=tok: y.dma_start(
                    y1d[tok:tok + 128, :], ystg_b[par][:, :]),
                waits=[("v", YS[("cv", qc, tt)])], sem=f"y_yb{par}")
        st1["prev_ttr"] = YS[("tr", qc, 3)]
    y1d_done = [(f"y_y1{p}", YS[("st", NQT - 1, 2 + p)]) for p in range(2)] + \
        [(f"y_yb{p}", YS[("stb", NQT - 1, 2 + p)]) for p in range(2)]
    p4_v = P.cnt["v"]
    p4_s = P.cnt["s"]
    p4_t = P.cnt["t"]

    sc_rt = contextlib.ExitStack()
    msrc = sc_rt.enter_context(nc.sbuf_tensor("msrc", [128, 64], F32))
    swf = sc_rt.enter_context(nc.sbuf_tensor("swf", [128, 256], F32))
    swi16 = sc_rt.enter_context(nc.sbuf_tensor("swi16", [128, 256], I16))
    trn_sb = sc_rt.enter_context(nc.sbuf_tensor("trn_sb", [32, 128], F32))
    s2sb = sc_rt.enter_context(nc.sbuf_tensor("s2sb", [128, 32], F32))
    wsb = sc_rt.enter_context(nc.sbuf_tensor("wsb", [16, 256], F32))
    selm_done = gate_ws[(NQT - 1, 3)]
    pos_mm = P.emit("t", lambda t_: [t_.matmul(
        ps_d[:, 8 * k:8 * k + 8], ltri_b[:, :], selm[:, 8 * k:8 * k + 8],
        start=True, stop=True) for k in range(16)][-1],
        waits=[("v", selm_done), ("v", YS[("ev", NQT - 1, 3)])])
    tot_mm = P.emit("t", lambda t_: [t_.matmul(
        ps_s[0][0:1, 512 + 8 * k:520 + 8 * k], ones128_b[:, :],
        selm[:, 8 * k:8 * k + 8], start=True, stop=True)
        for k in range(16)][-1], waits=[("s", st1["pss_guard_s"])])
    v_tr = P.emit("v", lambda v: v.tensor_copy(prow[0:1, 0:128],
                                               ps_s[0][0:1, 512:640]),
                  waits=[("t", tot_mm)])
    yp = P.emit("y", lambda y: y.dma_start(t16d[:, :], prow[0:1, 0:128]),
                waits=[("v", v_tr)], sem="y_ps")
    yp = P.emit("y", lambda y: y.dma_start(
        t16[:, :], t16d[:, :].rearrange("o (a b) -> (o a) b", a=16)),
        waits=[("y_ps", yp)], sem="y_ps")
    v_t16 = P.emit("v", lambda v: v.tensor_copy(t16b[:, :], t16[:, :]),
                   waits=[("y_ps", yp)])
    b_mm = P.emit("t", lambda t_: t_.matmul(
        ps_c[0][0:16, 0:8], ltri_b[0:16, 0:16], t16b[:, :], start=True,
        stop=True), waits=[("v", v_t16), ("v", st1["psc_guard_v"]),
                           ("s", st1["psc_guard_s"])])
    v_b16 = P.emit("v", lambda v: v.tensor_copy(t16[:, :],
                                                ps_c[0][0:16, 0:8]),
                   waits=[("t", b_mm)])
    yp = P.emit("y", lambda y: y.dma_start(
        b16d[:, :].rearrange("o (a b) -> (o a) b", a=16), t16[:, :]),
        waits=[("v", v_b16)], sem="y_ps")
    yp = P.emit("y", lambda y: y.dma_start(
        base_b[:, :], bass.AP(b16d, 0, [[0, 128], [1, 128]])),
        waits=[("y_ps", yp)], sem="y_ps")
    v_pos = P.emit("v", lambda v: v.tensor_tensor(
        posg[:, :], ps_d[:, 0:128], base_b[:, :], ALU.add),
        waits=[("y_ps", yp), ("t", pos_mm)])

    def slot_steps(k):
        kb = 40 * (k % 4)
        T1 = slt[:, kb:kb + 8]
        TA = slt[:, kb + 8:kb + 16]
        SF = slt[:, kb + 16:kb + 24]
        MQ = slt[:, kb + 24:kb + 32]
        SF2 = slt[:, kb + 32:kb + 40]
        s1f = sl12[:, 2 * (k % 4):2 * (k % 4) + 1]
        s2f = sl12[:, 2 * (k % 4) + 1:2 * (k % 4) + 2]
        return [
            lambda v: v.tensor_tensor(T1, posg[:, 8 * k:8 * k + 8],
                                      ecap_sb[:, :], ALU.add),
            lambda v: v.tensor_scalar(MQ, posg[:, 8 * k:8 * k + 8],
                                      CAP - 0.5, None, ALU.is_gt),
            lambda v: v.scalar_tensor_tensor(T1, MQ, 1.0e5, T1, ALU.mult,
                                             ALU.add),
            lambda v: v.tensor_scalar(TA, T1, BIGF, None, ALU.add),
            lambda v: v.scalar_tensor_tensor(SF, selm[:, 8 * k:8 * k + 8],
                                             -BIGF, TA, ALU.mult, ALU.add),
            lambda v: v.tensor_reduce(s1f, SF, axis=AX.X, op=ALU.min),
            lambda v: v.tensor_scalar(MQ, SF, s1f, None, ALU.is_equal),
            lambda v: v.scalar_tensor_tensor(SF2, MQ, BIGF, SF, ALU.mult,
                                             ALU.add),
            lambda v: v.tensor_reduce(s2f, SF2, axis=AX.X, op=ALU.min),
            lambda v: v.tensor_tensor(TA, MQ, wfull[:, 8 * k:8 * k + 8],
                                      ALU.mult),
            lambda v: v.reduce_sum(wv1s[:, k:k + 1], TA, axis=AX.X),
            lambda v: v.tensor_scalar(MQ, SF2, s2f, None, ALU.is_equal),
            lambda v: v.tensor_tensor(TA, MQ, wfull[:, 8 * k:8 * k + 8],
                                      ALU.mult),
            lambda v: v.reduce_sum(wv2s[:, k:k + 1], TA, axis=AX.X),
            lambda v: (v.tensor_scalar(slotf[:, k:k + 1], s1f, 6144.0,
                                       None, ALU.min),
                       v.tensor_scalar(slotf[:, 16 + k:17 + k], s2f, 6144.0,
                                       None, ALU.min))[-1],
        ]
    for k4 in range(0, 16, 4):
        steps = {k: slot_steps(k) for k in range(k4, k4 + 4)}
        lastv = {k: 0 for k in steps}
        for st_i in range(15):
            for k in steps:
                w = [("v", v_pos)] if st_i == 0 else [("v", lastv[k])]
                lastv[k] = P.emit("v", steps[k][st_i], waits=w)
    slots_v = P.cnt["v"]
    mbld = P.emit("v", lambda v: (
        v.tensor_copy(msrc[:, 0:32:2], tokf[:, :]),
        v.tensor_copy(msrc[:, 1:32:2], wv1s[:, :]),
        v.tensor_copy(msrc[:, 32:64:2], tokf[:, :]),
        v.tensor_copy(msrc[:, 33:64:2], wv2s[:, :]))[-1],
        waits=[("v", slots_v), ("g_io", gw_io)])
    tr1 = P.emit("t", lambda t_: t_.transpose(
        ps_d[0:32, 512:640], slotf[:, :], eye_f[:, :]),
        waits=[("v", slots_v), ("v", YS[("ev", NQT - 1, 3)])])
    ev1 = P.emit("v", lambda v: v.tensor_copy(trn_sb[:, :],
                                              ps_d[0:32, 512:640]),
                 waits=[("t", tr1)])
    yb_ = P.emit("y", lambda y: y.dma_start(
        bass.AP(sld, 0, [[128, 32], [1, 128]]), trn_sb[:, :]),
        waits=[("v", ev1)], sem="y_ps")
    vsw = 0
    for r_ in range(2):
        yb_ = P.emit("y", lambda y, r_=r_: y.dma_start(
            s2sb[:, 16 * r_:16 * r_ + 16],
            bass.AP(sld, TPC * r_, [[16, 128], [1, 16]])),
            waits=[("y_ps", yb_)], sem="y_ps")
        tr2 = P.emit("t", lambda t_, r_=r_: t_.transpose(
            ps_d[0:16, 640 + 128 * r_:768 + 128 * r_],
            s2sb[:, 16 * r_:16 * r_ + 16], eye_f[:, :]),
            waits=[("y_ps", yb_)])
        ev2 = P.emit("v", lambda v, r_=r_: v.tensor_copy(
            wsb[:, 128 * r_:128 * r_ + 128],
            ps_d[0:16, 640 + 128 * r_:768 + 128 * r_]),
            waits=[("t", tr2)])
        yb_ = P.emit("y", lambda y, r_=r_: y.dma_start(
            bass.AP(sld, 2 * TPC + TPC * r_, [[128, 16], [1, 128]]),
            wsb[:, 128 * r_:128 * r_ + 128]),
            waits=[("v", ev2)], sem="y_ps")
        yb_ = P.emit("y", lambda y, r_=r_: y.dma_start(
            swf[:, 128 * r_:128 * r_ + 128],
            bass.AP(sld, 2 * TPC + TPC * r_, [[0, 8], [128, 16], [1, 128]])),
            waits=[("y_ps", yb_)], sem="y_ps")
        vsw = P.emit("v", lambda v, r_=r_: v.tensor_copy(
            swi16[:, 128 * r_:128 * r_ + 128],
            swf[:, 128 * r_:128 * r_ + 128]),
            waits=[("y_ps", yb_)])
    MD = {"done": None}

    def emit_md_tail():
        P.emit("g", lambda g: g.dma_scatter_add(
            mdd[:, 0:2], msrc[:, 0:32].rearrange("p (j e) -> p j e", e=2),
            swi16[:, 0:128], TPC, TPC, 2, elem_step=64, queue_num=3),
            waits=[("v", vsw), ("v", mbld), ("g_pf", gw_pf)], sem="g_md")
        P.emit("g", lambda g: g.dma_scatter_add(
            mdd[:, 0:2], msrc[:, 32:64].rearrange("p (j e) -> p j e", e=2),
            swi16[:, 128:256], TPC, TPC, 2, elem_step=64, queue_num=3),
            sem="g_md")
        P.emit("g", lambda g: g.dma_start(mdc[:, :], mdd[:, 0:2]),
               waits=[("g_md", P.cnt["g_md"])], sem="g_mc")
        for e_ in range(NE):
            P.emit("g", lambda g, e_=e_: g.dma_start(
                bass.AP(mdc2, e_ * 2 * CAP, [[1, 2 * CAP]]),
                bass.AP(mdc, e_ * CAP * 2, [[2, 16], [32, 48], [1, 2]])),
                waits=[("g_mc", P.cnt["g_mc"])], sem="g_mc")
        return P.cnt["g_mc"]
    sc_rt.close()

    sc_p4.close()
    sc_ctx.close()
    sc_x.close()

    sc_f = contextlib.ExitStack()
    we1s = [sc_f.enter_context(nc.sbuf_tensor(f"we1s{p}", [128, 8 * 512],
                                              BF16)) for p in range(2)]
    we3s = [sc_f.enter_context(nc.sbuf_tensor(f"we3s{p}", [128, 8 * 512],
                                              BF16)) for p in range(2)]
    we2h = [sc_f.enter_context(nc.sbuf_tensor(f"we2h{p}", [128, NFC * 512],
                                              BF16)) for p in range(2)]
    h_pool = sc_f.enter_context(nc.sbuf_tensor("h_pool",
                                               [128, (NFC - 4) * CAP], BF16))
    h_side = [sc_f.enter_context(nc.sbuf_tensor(f"h_sd{p}", [128, 4 * CAP],
                                                BF16)) for p in range(2)]
    pool_fm = [sc_f.enter_context(nc.sbuf_tensor(f"pfm{p}", [128, 8 * CAP],
                                                 BF16)) for p in range(2)]
    ost = [sc_f.enter_context(nc.sbuf_tensor(f"ost{p}", [128, 3 * DM], F32))
           for p in range(2)]
    sa_p = [sc_f.enter_context(nc.sbuf_tensor(f"sa_p{p}", [128, 512], BF16))
            for p in range(2)]
    h1_p = [sc_f.enter_context(nc.sbuf_tensor(f"h1_p{p}", [128, 512], F32))
            for p in range(2)]
    mdw = [sc_f.enter_context(nc.sbuf_tensor(f"mdw{p}", [128, 2 * CAP],
                                              F32)) for p in range(2)]
    gi_p = [sc_f.enter_context(nc.sbuf_tensor(f"gi_p{p}", [128, 96], F32))
            for p in range(2)]
    tmpf0 = sc_f.enter_context(nc.sbuf_tensor("tmpf0", [128, 96], F32))
    tmpf = [tmpf0, tmpf0]
    gi16 = [sc_f.enter_context(nc.sbuf_tensor(f"gi16{p}", [128, 48], I16))
            for p in range(2)]
    si16 = [sc_f.enter_context(nc.sbuf_tensor(f"si16{p}", [128, 48], I16))
            for p in range(2)]
    space_guard = [("v", ln1_zlast), ("t", ln1_mmlast), ("v", p4_v),
                   ("s", p4_s)] + y1d_done

    NG = CAP // 128
    SUBS = [(0, 512), (512, CAP)]

    def emit_exp_loads(e, md_done):
        par_e = e % 2
        prevld["gix"] = gixL = P.emit("g", lambda g, e=e, par_e=par_e:
                                      g.dma_start(
            gi_p[par_e][:, :],
            bass.AP(mdc2, e * 2 * CAP, [[0, 8], [96, 16], [1, 96]])),
            waits=[("g_mc", md_done), ("v", VC.get(e - 2, 0)),
                   ("g_ix", prevld["gix"])] +
                  (y1d_done if e == 0 else []), sem="g_ix")
        GIX[e] = gixL
        prevld["wbb"] = wbbL = P.emit("g", lambda g, e=e, par_e=par_e:
                                      g.dma_start(
            mdw[par_e][:, :],
            bass.AP(mdc, e * CAP * 2, [[0, 128], [1, 2 * CAP]])),
            waits=[("v", HT_WBB.get(e - 2, 0)), ("g_wb", prevld["wbb"]),
                   ("g_mc", md_done)] +
                  (y1d_done if e == 0 else []), sem="g_wb")
        WBB[e] = wbbL
        vc1 = P.emit("v", lambda v, par_e=par_e: (
            v.tensor_scalar(tmpf[par_e][:, 0:48], gi_p[par_e][:, 0:96:2],
                            0.0, None, ALU.max),
            v.tensor_scalar(tmpf[par_e][:, 48:96], gi_p[par_e][:, 0:96:2],
                            0.0, None, ALU.is_lt))[-1],
            waits=[("g_ix", gixL)])
        vc2 = P.emit("v", lambda v, par_e=par_e: (
            v.tensor_copy(gi16[par_e][:, :], tmpf[par_e][:, 0:48]),
            v.scalar_tensor_tensor(tmpf[par_e][:, 48:96],
                                   tmpf[par_e][:, 48:96],
                                   float(TPC + 1), gi_p[par_e][:, 0:96:2],
                                   ALU.mult, ALU.add))[-1],
            waits=[("v", vc1)])
        VC[e] = vc = P.emit("v", lambda v, par_e=par_e: v.tensor_copy(
            si16[par_e][:, :], tmpf[par_e][:, 48:96]),
            waits=[("v", vc2), ("g_sc", SCH.get((e - 2, 1), 0))])
        GA[e] = P.emit("g", lambda g, par_e=par_e: g.dma_gather(
            pool_fm[par_e][:, :].rearrange("p (c n) -> p c n", c=NDC),
            y1d[:, :], gi16[par_e][:, :], CAP, CAP, DM,
            transpose=True, queue_num=1),
            waits=[("v", vc), ("t", POOL_LAST.get(e - 2, 0))] +
                  (y1d_done if e == 0 else []),
            sem=f"g_ga{par_e}")

    FW = {}
    ab_last = {}
    o_done_t = {}
    o_evac_v = {}
    POOL_LAST = {}
    VC = {}
    GA = {}
    HT_WBB = {}
    SCH = {}
    shared_ab = {}
    RDST = {}
    GIX = {}
    WBB = {}
    prevld = {"gix": 0, "wbb": 0}
    sc_add_T = 0
    rd_init = 0
    ffn_s_last, ffn_h1_last = 0, 0
    for ei in range(NE + 1):
        e = ei - 1
        if DBG_SKIP_ROUTED and ei >= DBG_SKIP_ROUTED:
            continue
        s1d = ws1_d[:, :] if e < 0 else we1_d[e, :, :]
        s3d = ws3_d[:, :] if e < 0 else we3_d[e, :, :]
        s2d = ws2_d[:, :] if e < 0 else we2_d[e, :, :]

        def emit_w13(key, sl, wguard, extra=(), s1d=s1d, s3d=s3d):
            par = sl % 2
            prev13 = P.cnt.get("g_we13", 0)
            FW[key + ("w1",)] = P.emit("g", lambda g, sl=sl, par=par,
                                       s1d=s1d: g.dma_start(
                we1s[par][:, :].rearrange("p (c n) -> p c n", c=8),
                s1d[:, sl * 512:(sl + 1) * 512]
                .rearrange("(c p) n -> p c n", p=128)),
                waits=[("t", wguard), ("g_we13", prev13)] + list(extra),
                sem="g_we13")
            FW[key + ("w3",)] = P.emit("g", lambda g, sl=sl, par=par,
                                       s3d=s3d: g.dma_start(
                we3s[par][:, :].rearrange("p (c n) -> p c n", c=8),
                s3d[:, sl * 512:(sl + 1) * 512]
                .rearrange("(c p) n -> p c n", p=128)), sem="g_we13")

        if ei >= 1:
            for sl in range(2):
                emit_w13((ei, sl), sl,
                         ab_last.get((ei - 1, sl + 2),
                                     shared_ab.get((NQT - 1, sl + 2), 0)))
        for hf in range(2):
            prev2 = P.cnt.get("g_we2", 0)
            FW[(ei, hf, "w2")] = P.emit("g", lambda g, s2d=s2d, hf=hf:
                                        g.dma_start(
                we2h[hf][:, :].rearrange("p (c n) -> p c n", c=16),
                s2d[:, hf * 512:(hf + 1) * 512]
                .rearrange("(c p) n -> p c n", p=128)),
                waits=[("t", o_done_t.get(ei - 1, ln1_mmlast)),
                       ("g_we2", prev2)] +
                      (space_guard if ei == 0 else []), sem="g_we2")

        if ei == 0:
            ht = {}
            sls = {}
            for tt in range(NQT):
                if tt == 1:
                    MD["done"] = emit_md_tail()
                for sl in range(2):
                    emit_w13((0, tt, sl), sl,
                             shared_ab.get((tt - 1, sl + 2), 0),
                             space_guard if tt == 0 else ())
                for fc in range(NFC):
                    if fc in (4, 8):
                        emit_w13((0, tt, fc // 4 + 1), fc // 4 + 1,
                                 shared_ab[(tt, fc // 4 - 1)])
                    par = fc % 2

                    def abf(t_, fc=fc, par=par, tt=tt):
                        for di in range(NDC):
                            t_.matmul(ps_s[par][:, 0:QT],
                                      we1s[(fc // 4) % 2][:, di * 512 +
                                                          (fc % 4) * 128:
                                                          di * 512 +
                                                          (fc % 4 + 1) * 128],
                                      y1[di][:, tt * QT:(tt + 1) * QT],
                                      start=(di == 0), stop=(di == NDC - 1))
                        for di in range(NDC):
                            ii = t_.matmul(ps_s[par][:, QT:2 * QT],
                                           we3s[(fc // 4) % 2][:, di * 512 +
                                                               (fc % 4) * 128:
                                                               di * 512 +
                                                               (fc % 4 + 1) * 128],
                                           y1[di][:, tt * QT:(tt + 1) * QT],
                                           start=(di == 0),
                                           stop=(di == NDC - 1))
                        return ii
                    w = [("g_we13", FW[(0, tt, fc // 4, "w3")]),
                         ("v", ht.get((tt, fc - 2), 0)),
                         ("s", sls.get((tt, fc - 2), 0)),
                         ("v", vsw if tt == 0 and fc < 2 else 0),
                         ("s", st1.get("pss_guard_s", 0)
                          if tt == 0 and fc < 2 else 0)]
                    ab = P.emit("t", abf, waits=w)
                    if fc % 4 == 3:
                        shared_ab[(tt, fc // 4)] = P.cnt["t"]
                    sls[(tt, fc)] = sl_ = P.emit(
                        "s", lambda s, par=par: s.activation(
                            sa_p[par][:, :], ps_s[par][:, 0:QT], AF.Silu),
                        waits=[("t", ab)])
                    ffn_s_last = sl_
                    h1 = P.emit("v", lambda v, par=par: v.tensor_tensor(
                        h1_p[par][:, :], sa_p[par][:, :],
                        ps_s[par][:, QT:2 * QT], ALU.mult),
                        waits=[("s", sl_)])
                    ffn_h1_last = h1
                    hdst_s = h_side[0] if fc < 4 else h_pool
                    fcb_s = fc if fc < 4 else fc - 4
                    ht[(tt, fc)] = P.emit(
                        "v", lambda v, fcb_s=fcb_s, fc=fc, hdst_s=hdst_s:
                        v.tensor_copy(
                            hdst_s[:, fcb_s * CAP:fcb_s * CAP + 512],
                            h1_p[fc % 2][:, :]),
                        waits=[("v", h1),
                               ("t", o_done_t.get((0, tt - 1), 0))])
                for sub in range(4):
                    tok = tt * QT + sub * 128
                    for hf in range(2):
                        def omf(t_, tt=tt, sub=sub, hf=hf):
                            for fc in range(NFC):
                                hsrc = h_side[0] if fc < 4 else h_pool
                                fcb = fc if fc < 4 else fc - 4
                                ii = t_.matmul(
                                    ps_c[sub % 2][:, 0:QT],
                                    hsrc[:, fcb * CAP + sub * 128:
                                         fcb * CAP + (sub + 1) * 128],
                                    we2h[hf][:, fc * 512:(fc + 1) * 512],
                                    start=(fc == 0), stop=(fc == NFC - 1))
                            return ii
                        w = [("v", ht[(tt, NFC - 1)]),
                             ("g_we2", FW[(ei, hf, "w2")]),
                             ("v", o_evac_v.get((tt, sub, 0), 0)
                              if hf == 1 else 0),
                             ("v", vsw if tt == 0 and sub < 2 else 0),
                             ("v", o_evac_v.get((tt, sub - 2),
                                  o_evac_v.get((tt - 1, sub + 2), 0)))]
                        ot_t = P.emit("t", omf, waits=w)
                        o_evac_v[(tt, sub, hf)] = P.emit(
                            "v", lambda v, sub=sub, hf=hf: v.tensor_copy(
                                ost[sub % 2][:, hf * 512:(hf + 1) * 512],
                                ps_c[sub % 2][:, 0:QT]),
                            waits=[("t", ot_t),
                                   (f"y_rd{sub % 2}", RDST.get(sub % 2, 0)
                                    if hf == 0 else 0)])
                    o_evac_v[(tt, sub)] = o_evac_v[(tt, sub, 1)]
                    rd_init = P.emit(
                        "y", lambda y, tok=tok, sub=sub: y.dma_start(
                            rdd[tok:tok + 128, :], ost[sub % 2][:, 0:DM]),
                        waits=[("v", o_evac_v[(tt, sub, 1)])],
                        sem=f"y_rd{sub % 2}")
                    RDST[sub % 2] = rd_init
                o_done_t[(0, tt)] = P.cnt["t"]
            o_done_t[0] = P.cnt["t"]
        else:
            par_e = e % 2
            if e == 0:
                md_done = MD["done"]
                emit_exp_loads(0, md_done)
            ps_guard_s, ps_guard_v = ffn_s_last, ffn_h1_last
            ht = {}
            sls = {}
            si = 0
            for fc in range(NFC):
                if fc in (4, 8) and True:
                    emit_w13((ei, fc // 4 + 1), fc // 4 + 1,
                             ab_last[(ei, fc // 4 - 1)])
                for (lo, hi) in SUBS:
                    n = hi - lo
                    par = si % 2
                    si += 1

                    def abf(t_, fc=fc, lo=lo, hi=hi, n=n, par=par,
                            par_e=par_e):
                        for di in range(NDC):
                            t_.matmul(ps_s[par][:, 0:n],
                                      we1s[(fc // 4) % 2][:, di * 512 +
                                                          (fc % 4) * 128:
                                                          di * 512 +
                                                          (fc % 4 + 1) * 128],
                                      pool_fm[par_e][:, di * CAP + lo:
                                                     di * CAP + hi],
                                      start=(di == 0), stop=(di == NDC - 1))
                        for di in range(NDC):
                            ii = t_.matmul(ps_s[par][:, QT:QT + n],
                                           we3s[(fc // 4) % 2][:, di * 512 +
                                                               (fc % 4) * 128:
                                                               di * 512 +
                                                               (fc % 4 + 1) * 128],
                                           pool_fm[par_e][:, di * CAP + lo:
                                                          di * CAP + hi],
                                           start=(di == 0),
                                           stop=(di == NDC - 1))
                        return ii
                    w = [("g_we13", FW[(ei, fc // 4, "w3")]),
                         ("v", ht.get(si - 2, 0)), ("s", sls.get(si - 2, 0)),
                         (f"g_ga{par_e}", GA[e] if si == 1 else 0),
                         ("s", ps_guard_s if si <= 2 else 0),
                         ("v", ps_guard_v if si <= 2 else 0)]
                    ab = P.emit("t", abf, waits=w)
                    if fc in (3, 7, 11, NFC - 1) and lo != 0:
                        ab_last[(ei, fc // 4)] = P.cnt["t"]
                    sls[si] = sl_ = P.emit(
                        "s", lambda s, par=par, n=n: s.activation(
                            sa_p[par][:, 0:n], ps_s[par][:, 0:n], AF.Silu),
                        waits=[("t", ab)])
                    ffn_s_last = sl_
                    h1 = P.emit("v", lambda v, par=par, n=n: v.tensor_tensor(
                        h1_p[par][:, 0:n], sa_p[par][:, 0:n],
                        ps_s[par][:, QT:QT + n], ALU.mult),
                        waits=[("s", sl_)])
                    ffn_h1_last = h1
                    hdst = h_side[par_e] if fc < 4 else h_pool
                    fcb = fc if fc < 4 else fc - 4
                    ht[si] = P.emit(
                        "v", lambda v, fcb=fcb, lo=lo, hi=hi, par=par,
                        hdst=hdst, par_e=par_e: v.tensor_tensor(
                            hdst[:, fcb * CAP + lo:fcb * CAP + hi],
                            h1_p[par][:, 0:hi - lo],
                            mdw[par_e][:, 2 * lo + 1:2 * hi:2],
                            ALU.mult),
                        waits=[("v", h1),
                               ("g_wb", WBB[e] if si == 1 else 0),
                               ("t", o_done_t.get(ei - 2, 0)
                                if si == 1 else 0),
                               ("t", o_done_t.get(ei - 1, 0)
                                if si == 9 else 0)])
            POOL_LAST[e] = P.cnt["t"]
            h_done = ht[si]
            HT_WBB[e] = P.cnt["v"]
            if e + 1 < NE:
                emit_exp_loads(e + 1, md_done)
            for g6 in range(NG):
                hhalf = g6 // 3
                col = (g6 % 3) * DM
                for hf in range(2):
                    def omf(t_, g6=g6, hf=hf, par_e=par_e):
                        for fc in range(NFC):
                            hsrc = h_side[par_e] if fc < 4 else h_pool
                            fcb = fc if fc < 4 else fc - 4
                            ii = t_.matmul(
                                ps_c[g6 % 2][:, 0:QT],
                                hsrc[:, fcb * CAP + g6 * 128:
                                     fcb * CAP + (g6 + 1) * 128],
                                we2h[hf][:, fc * 512:(fc + 1) * 512],
                                start=(fc == 0), stop=(fc == NFC - 1))
                        return ii
                    w = [("v", h_done), ("g_we2", FW[(ei, hf, "w2")]),
                         ("v", o_evac_v.get((ei, g6, 0), 0)
                          if hf == 1 else 0),
                         ("v", o_evac_v.get((ei, g6 - 2),
                              o_evac_v.get((ei - 1, g6 + NG - 2),
                                           o_evac_v.get((NQT - 1, g6 + 2),
                                                        0))))]
                    ot_t = P.emit("t", omf, waits=w)
                    o_evac_v[(ei, g6, hf)] = P.emit(
                        "v", lambda v, g6=g6, hf=hf, col=col, hhalf=hhalf:
                        v.tensor_copy(
                            ost[hhalf][:, col + hf * 512:col + hf * 512 + 512],
                            ps_c[g6 % 2][:, 0:QT]),
                        waits=[("t", ot_t),
                               ("g_sc", SCH.get((e - 1, hhalf), 0)
                                if g6 % 3 == 0 else 0),
                               ("y_rd0", RDST.get(0, 0)
                                if e == 0 and g6 % 3 == 0 else 0),
                               ("y_rd1", RDST.get(1, 0)
                                if e == 0 and g6 % 3 == 0 else 0)])
                o_evac_v[(ei, g6)] = o_evac_v[(ei, g6, 1)]
                if g6 % 3 == 2:
                    sc_add_T = P.emit(
                        "g", lambda g, par_e=par_e, hhalf=hhalf:
                        g.dma_scatter_add(
                            rdd[:, :],
                            ost[hhalf][:, :].rearrange("p (j e2) -> p j e2",
                                                       j=3),
                            si16[par_e][:, 24 * hhalf:24 * hhalf + 24],
                            384, 384, DM, queue_num=2),
                        waits=[("v", o_evac_v[(ei, g6, 1)]),
                               ("g_sc", sc_add_T),
                               ("y_rd0", RDST.get(0, 0) if e == 0 else 0),
                               ("y_rd1", RDST.get(1, 0) if e == 0 else 0)],
                        sem="g_sc")
                    SCH[(e, hhalf)] = sc_add_T
            o_done_t[ei] = P.cnt["t"]
    adds_done = sc_add_T
    sc_f.close()
    sc_l2 = contextlib.ExitStack()
    NPL = 4
    rt2 = [sc_l2.enter_context(nc.sbuf_tensor(f"rt2{p}", [128, DM], F32))
           for p in range(NPL)]
    y1t2 = [sc_l2.enter_context(nc.sbuf_tensor(f"y1t2{p}", [128, DM], F32))
            for p in range(NPL)]
    zt2 = [sc_l2.enter_context(nc.sbuf_tensor(f"zt2{p}", [128, DM], F32))
           for p in range(NPL)]
    zq2 = [sc_l2.enter_context(nc.sbuf_tensor(f"zq2{p}", [128, DM], F32))
           for p in range(NPL)]
    lrow = sc_l2.enter_context(nc.sbuf_tensor("lrow", [128, 32], F32))
    y2o = [sc_l2.enter_context(nc.sbuf_tensor(f"y2o{p}", [128, DM], F32))
           for p in range(NPL)]
    L2 = {}
    prev_st = {p: 0 for p in range(NPL)}
    for ti in range(16):
        par = ti % NPL
        tok = ti * 128
        ld1 = P.emit("y", lambda y, par=par, tok=tok: y.dma_start(
            rt2[par][:, :], rdd[tok:tok + 128, :]),
            waits=[("g_sc", adds_done), ("v", L2.get((ti - 2, "z"), 0)),
                   ("y_rd0", RDST.get(0, 0)), ("y_rd1", RDST.get(1, 0)),
                   ("t", o_done_t.get(NE, o_done_t.get(0, 0))),
                   ("v", ffn_h1_last), ("s", ffn_s_last)] + y1d_done,
            sem=f"y_l2{'a' if par % 2 == 0 else 'b'}")
        ld2 = P.emit("y", lambda y, par=par, tok=tok: y.dma_start(
            y1t2[par][:, :], y1f[tok:tok + 128, :]),
            sem=f"y_l2{'a' if par % 2 == 0 else 'b'}")
        L2[(ti, "z")] = P.emit("v", lambda v, par=par: v.scalar_tensor_tensor(
            zt2[par][:, :], rt2[par][:, :], 1.0 / 3.0, y1t2[par][:, :],
            ALU.mult, ALU.add),
            waits=[(f"y_l2{'a' if par % 2 == 0 else 'b'}", ld2),
                   ("v", L2.get((ti - NPL, "fin"), 0))])
        L2[(ti, "m")] = P.emit("v", lambda v, par=par: v.reduce_sum(
            lrow[:, par * 8:par * 8 + 1], zt2[par][:, :], axis=AX.X),
            waits=[("v", L2[(ti, "z")])])
        L2[(ti, "sq")] = P.emit("s", lambda s, par=par: s.activation(
            zq2[par][:, :], zt2[par][:, :], AF.Square),
            waits=[("v", L2[(ti, "z")])])
        L2[(ti, "s2")] = P.emit("v", lambda v, par=par: v.reduce_sum(
            lrow[:, par * 8 + 1:par * 8 + 2], zq2[par][:, :],
            axis=AX.X), waits=[("s", L2[(ti, "sq")])])
        b8 = par * 8
        r1_ = P.emit("v", lambda v, b8=b8: v.tensor_scalar(
            lrow[:, b8 + 2:b8 + 3], lrow[:, b8:b8 + 1], 1.0 / DM, None,
            ALU.mult), waits=[("v", L2[(ti, "s2")])])
        r2_ = P.emit("v", lambda v, b8=b8: (
            v.tensor_scalar(lrow[:, b8 + 3:b8 + 4], lrow[:, b8 + 1:b8 + 2],
                            1.0 / DM, None, ALU.mult),
            v.tensor_tensor(lrow[:, b8 + 4:b8 + 5], lrow[:, b8 + 2:b8 + 3],
                            lrow[:, b8 + 2:b8 + 3], ALU.mult))[-1],
            waits=[("v", r1_)])
        r3_ = P.emit("v", lambda v, b8=b8: v.tensor_tensor(
            lrow[:, b8 + 5:b8 + 6], lrow[:, b8 + 3:b8 + 4],
            lrow[:, b8 + 4:b8 + 5], ALU.subtract), waits=[("v", r2_)])
        r4_ = P.emit("s", lambda s, b8=b8: s.activation(
            lrow[:, b8 + 6:b8 + 7], lrow[:, b8 + 5:b8 + 6], AF.Sqrt,
            bias=1e-5), waits=[("v", r3_)])
        r5_ = P.emit("v", lambda v, b8=b8: v.reciprocal(
            lrow[:, b8 + 7:b8 + 8], lrow[:, b8 + 6:b8 + 7]),
            waits=[("s", r4_)])
        L2[(ti, "fin")] = P.emit("v", lambda v, par=par, b8=b8:
                                 v.tensor_scalar(
            y2o[par][:, :], zt2[par][:, :], lrow[:, b8 + 2:b8 + 3],
            lrow[:, b8 + 7:b8 + 8], ALU.subtract, ALU.mult),
            waits=[("v", r5_), (f"y_st{par}", prev_st[par])])
        prev_st[par] = P.emit("y", lambda y, par=par, tok=tok: y.dma_start(
            out_d[tok:tok + 128, :], y2o[par][:, :]),
            waits=[("v", L2[(ti, "fin")])], sem=f"y_st{par}")
    P.emit("y", lambda y: y.nop(),
           waits=[(f"y_st{p}", prev_st[p]) for p in range(NPL)], inc=False)

    P.finalize(nc, sems)
    sc_l2.close()
    stk.close()
    return nc


def _host_inputs(inputs):
    x = np.ascontiguousarray(np.asarray(inputs["L0_attn_hidden_states"],
                                        np.float32))
    up = np.asarray(inputs["L0_attn_up"], np.float32).reshape(-1)
    uz = np.asarray(inputs["L0_attn_uz"], np.float32).reshape(-1)
    upz = np.zeros((128, 512), np.float32)
    r = np.arange(128)
    for dc in range(8):
        h = 2 * dc + r // 64
        upz[r, dc * 64 + h] = up[dc * 128 + r]
        upz[r, dc * 64 + 32 + h] = uz[dc * 128 + r]
    wg = np.asarray(inputs["L0_ffn_Wg"], np.float32)
    wg_m = np.ascontiguousarray(wg.reshape(8, 128, 8).transpose(1, 0, 2)
                                .reshape(128, 64))

    def colmat(name):
        a = np.asarray(inputs[name], np.float32).reshape(-1)
        return np.ascontiguousarray(a.reshape(8, 128).T)

    def arr(name):
        return np.ascontiguousarray(np.asarray(inputs[name], np.float32))

    ltri = (np.arange(128)[:, None] < np.arange(128)[None, :]
            ).astype(np.float32)
    ecap = np.broadcast_to((np.arange(8) * 768).astype(np.float32),
                           (128, 8)).copy()
    shared = {
        "eye": np.eye(128, dtype=np.float32),
        "ltri": ltri, "ecap": ecap,
        "wq": arr("L0_attn_wq"), "wk": arr("L0_attn_wk"),
        "wv": arr("L0_attn_wv"), "wo": arr("L0_attn_wo"),
        "wfq": arr("L0_attn_wfq"), "wfg": arr("L0_attn_wfg"),
        "bq_m": colmat("L0_attn_bq"), "bk_m": colmat("L0_attn_bk"),
        "bo_m": colmat("L0_attn_bo"), "bfq_m": colmat("L0_attn_bfq"),
        "bfg_m": colmat("L0_attn_bfg"),
        "bv_m": colmat("L0_attn_bv"),
        "bv_r": np.ascontiguousarray(
            np.asarray(inputs["L0_attn_bv"], np.float32).reshape(1, -1)),
        "upz_m": upz, "wg_m": wg_m,
        "ws1": arr("L0_ffn_Ws1"), "ws3": arr("L0_ffn_Ws3"),
        "ws2": arr("L0_ffn_Ws2"), "we1": arr("L0_ffn_We1"),
        "we3": arr("L0_ffn_We3"), "we2": arr("L0_ffn_We2"),
    }
    in_maps = []
    for c in range(8):
        b, off = c // 2, (c % 2) * TPC
        m = dict(shared)
        m["xr"] = np.ascontiguousarray(
            np.concatenate([x[b, off:], x[b, :off]], axis=0))
        m["jv"] = np.ascontiguousarray(
            np.concatenate([np.arange(off, S), np.arange(0, off)])
            .astype(np.float32).reshape(1, S))
        in_maps.append(m)
    return in_maps


_NC_CACHE = {}


def kernel(**inputs):
    if "nc" not in _NC_CACHE:
        nc_ = build_full()
        mybir.codegen_inst_isa_subclasses(nc_)
        _NC_CACHE["nc"] = nc_
    nc = _NC_CACHE["nc"]
    in_maps = _host_inputs(inputs)
    res = run_bass_kernel_spmd(nc, in_maps, core_ids=list(range(8)))
    out = np.empty((B, S, DM), np.float32)
    for c in range(8):
        b, off = c // 2, (c % 2) * TPC
        out[b, off:off + TPC] = res.results[c]["out"]
    return out


# version 14 (final)
# speedup vs baseline: 1.1059x; 1.1059x over previous
import contextlib
import numpy as np
import concourse.bass as bass
import concourse.mybir as mybir
from concourse.library_config import mlp as MLP_LIB
from concourse.bass_utils import run_bass_kernel_spmd

F32 = mybir.dt.float32
BF16 = mybir.dt.bfloat16
AF = mybir.ActivationFunctionType
ALU = mybir.AluOpType
AX = mybir.AxisListType

B, S, DM, H, D = 4, 4096, 1024, 16, 64
I32 = mybir.dt.int32
I16 = mybir.dt.int16
TPC, FF, NE, QT = 2048, 2048, 8, 512
NQT, NKC, NDC, NFC, NXT = TPC // QT, S // 128, DM // 128, FF // 128, S // 128
CAP = 768
DBG_SKIP_ROUTED = False
BIGF = 1.0e6

ENGS = ("t", "s", "v", "g", "y")


class Prog:

    def __init__(self):
        self.recs = {e: [] for e in ENGS}
        self.cnt = {}

    def emit(self, eng, fn, waits=(), inc=True, sem=None):
        semn = sem or eng
        inc_k = 16 if eng in ("g", "y") else 1
        ws = [(e, v) for (e, v) in waits if v and v > 0]
        self.recs[eng].append((fn, ws, semn if inc else None, inc_k))
        if inc:
            self.cnt[semn] = self.cnt.get(semn, 0) + inc_k
        return self.cnt.get(semn, 0)

    def finalize(self, nc, sems):
        def run(eng_obj, name):
            lw = {}
            for fn, ws, semn, inc_k in self.recs[name]:
                for se, val in ws:
                    if val > lw.get(se, 0):
                        eng_obj.wait_ge(sems[se], val)
                        lw[se] = val
                inst = fn(eng_obj)
                if semn is not None:
                    inst.then_inc(sems[semn], inc_k)

        with nc.Block() as block:
            block.tensor(lambda e: run(e, "t"))
            block.scalar(lambda e: run(e, "s"))
            block.vector(lambda e: run(e, "v"))
            block.gpsimd(lambda e: run(e, "g"))
            block.sync(lambda e: run(e, "y"))


def build_full():
    nc = bass.Bass("TRN2", target_bir_lowering=False,
                   num_swdge_queues=4)

    dt_ = nc.dram_tensor
    xr = dt_("xr", [S, DM], F32, kind="ExternalInput")
    jv_d = dt_("jv", [1, S], F32, kind="ExternalInput")
    eye_d = dt_("eye", [128, 128], F32, kind="ExternalInput")
    wq_d = dt_("wq", [DM, DM], F32, kind="ExternalInput")
    wk_d = dt_("wk", [DM, DM], F32, kind="ExternalInput")
    wv_d = dt_("wv", [DM, DM], F32, kind="ExternalInput")
    wo_d = dt_("wo", [DM, DM], F32, kind="ExternalInput")
    wfq_d = dt_("wfq", [DM, DM], F32, kind="ExternalInput")
    wfg_d = dt_("wfg", [DM, DM], F32, kind="ExternalInput")
    bq_d = dt_("bq_m", [128, 8], F32, kind="ExternalInput")
    bk_d = dt_("bk_m", [128, 8], F32, kind="ExternalInput")
    bo_d = dt_("bo_m", [128, 8], F32, kind="ExternalInput")
    bfq_d = dt_("bfq_m", [128, 8], F32, kind="ExternalInput")
    bfg_d = dt_("bfg_m", [128, 8], F32, kind="ExternalInput")
    bv_d = dt_("bv_m", [128, 8], F32, kind="ExternalInput")
    bvr_d = dt_("bv_r", [1, DM], F32, kind="ExternalInput")
    upz_d = dt_("upz_m", [128, 512], F32, kind="ExternalInput")
    wg_d = dt_("wg_m", [128, 64], F32, kind="ExternalInput")
    ws1_d = dt_("ws1", [DM, FF], F32, kind="ExternalInput")
    ws3_d = dt_("ws3", [DM, FF], F32, kind="ExternalInput")
    ws2_d = dt_("ws2", [FF, DM], F32, kind="ExternalInput")
    we1_d = dt_("we1", [NE, DM, FF], F32, kind="ExternalInput")
    we3_d = dt_("we3", [NE, DM, FF], F32, kind="ExternalInput")
    we2_d = dt_("we2", [NE, FF, DM], F32, kind="ExternalInput")
    out_d = dt_("out", [TPC, DM], F32, kind="ExternalOutput")
    rbd = dt_("rbd", [1, 2 * QT], F32, kind="Internal")
    y1d = dt_("y1d", [TPC, DM], BF16, kind="Internal")
    y1f = dt_("y1f", [TPC, DM], F32, kind="Internal")
    rdd = dt_("rdd", [TPC + 128, DM], F32, kind="Internal")
    mdd = dt_("mdd", [NE * CAP + 128, 64], F32, kind="Internal")
    mdc = dt_("mdc", [NE * CAP + 128, 2], F32, kind="Internal")
    mdc2 = dt_("mdc2", [NE, 2 * CAP], F32, kind="Internal")
    sld = dt_("sld", [1, 4 * TPC], F32, kind="Internal")
    t16d = dt_("t16d", [1, 128], F32, kind="Internal")
    b16d = dt_("b16d", [1, 128], F32, kind="Internal")
    ltri_d = dt_("ltri", [128, 128], F32, kind="ExternalInput")
    ecap_d = dt_("ecap", [128, 8], F32, kind="ExternalInput")

    al = nc.alloc_sbuf_tensor
    eye_f = al("eye_f", [128, 128], F32)
    eye_b = al("eye_b", [128, 128], BF16)
    ones1_b = al("ones1_b", [1, 128], BF16)
    ones128_b = al("ones128_b", [128, 1], BF16)
    cpool = al("cpool", [128, 2], F32)
    bq_sb = al("bq_sb", [128, 8], F32)
    bk_sb = al("bk_sb", [128, 8], F32)
    bo_sb = al("bo_sb", [128, 8], F32)
    bfq_sb = al("bfq_sb", [128, 8], F32)
    bfg_sb = al("bfg_sb", [128, 8], F32)
    cbias = al("cbias", [128, 8], F32)
    bv_sb = al("bv_sb", [128, 8], F32)
    wg_b = al("wg_b", [128, 64], BF16)
    xm_sb = al("xm_sb", [128, 8], F32)
    xm_b = al("xm_b", [128, 8], BF16)
    glo_sb = al("glo_sb", [128, 8], F32)
    glo_b = al("glo_b", [128, 8], BF16)
    rowsF = al("rowsF", [1, 1024], F32)
    recr_f = [rowsF[0:1, 0:512], rowsF[0:1, 512:1024]]
    rowsB = al("rowsB", [1, 2048], BF16)
    mrow_b, irow_b = rowsB[0:1, 0:512], rowsB[0:1, 512:1024]
    recr = [rowsB[0:1, 1024:1536], rowsB[0:1, 1536:2048]]
    ltri_b = al("ltri_b", [128, 128], BF16)
    ecap_sb = al("ecap_sb", [128, 8], F32)
    selm = al("selm", [128, 128], BF16)
    wfull = al("wfull", [128, 128], F32)
    posg = al("posg", [128, 128], F32)
    base_b = al("base_b", [128, 128], F32)
    t16 = al("t16", [16, 8], F32)
    t16b = al("t16b", [16, 8], BF16)
    prow = al("prow", [1, 256], F32)
    slt = al("slt", [128, 160], F32)
    sl12 = al("sl12", [128, 8], F32)
    slotf = al("slotf", [128, 32], F32)
    tokf = al("tokf", [128, 16], F32)
    pre2 = al("pre2", [128, 98], F32)
    wv1s = al("wv1s", [128, 16], F32)
    wv2s = al("wv2s", [128, 16], F32)
    mq = [al(f"mq{c}", [128, TPC], BF16) for c in range(NDC)]
    y1 = mq

    ap_ = nc.alloc_psum_tensor
    ps_s = [ap_(f"ps_s{p}", [128, 2 * QT], F32) for p in range(2)]
    ps_c = [ap_(f"ps_c{i}", [128, QT], F32) for i in range(2)]
    sc_ptr = contextlib.ExitStack()
    ps_tr = [sc_ptr.enter_context(nc.psum_tensor(f"ps_tr{p}", [128, 1024],
                                                 BF16)) for p in range(2)]

    stk = contextlib.ExitStack()
    STREAMS = ["t", "s", "v", "y_jv", "y_eye", "y_x0", "y_x1", "y_st0",
               "y_st1", "y_st2", "y_st3", "g_sm", "g_j", "g_wb", "g_upz", "g_bv", "g_kv0",
               "g_kv1", "g_c30", "g_c31", "g_wr8", "g_st0", "g_st1",
               "g_we13", "g_we2", "y_rb", "y_bv", "g_wv", "g_pf", "y_ps",
               "g_sc", "g_ix", "g_ga0", "g_ga1", "y_rd0", "y_rd1", "y_y10",
               "y_y11",
               "y_yb0", "y_yb1", "g_md", "g_mc", "g_io", "y_l2a", "y_l2b", "y_l2c", "y_l2d"]
    sems = {e: stk.enter_context(nc.semaphore(f"sem_{e}")) for e in STREAMS}
    P = Prog()
    nc.const_aps.register(F32, 1e-5, cpool[:, 0:1])

    def chunked(dr, c=8):
        return dr.rearrange("(c p) n -> p c n", p=128)

    def wsl(buf, di, dc):
        return buf[:, di * DM + dc * 128: di * DM + (dc + 1) * 128]

    sc_x = contextlib.ExitStack()
    x_fm = [sc_x.enter_context(nc.sbuf_tensor(f"x_fm{c}", [128, S], BF16))
            for c in range(NDC)]
    sc_ctx = contextlib.ExitStack()
    ctx_fm = [sc_ctx.enter_context(nc.sbuf_tensor(f"ctx{c}", [128, TPC], BF16))
              for c in range(NDC)]
    sc_kaug = contextlib.ExitStack()
    k_augA = sc_kaug.enter_context(nc.sbuf_tensor("k_augA", [128, S], BF16))
    k_augB = sc_kaug.enter_context(nc.sbuf_tensor("k_augB", [128, S], BF16))

    sc_p0 = contextlib.ExitStack()
    poolF = sc_p0.enter_context(nc.sbuf_tensor("poolF", [1, S], F32))
    poolB = sc_p0.enter_context(nc.sbuf_tensor("poolB", [65, S], BF16))
    stage_a = [sc_p0.enter_context(nc.sbuf_tensor(f"stg{p}", [128, DM], F32))
               for p in range(2)]
    xb_t = [sc_p0.enter_context(nc.sbuf_tensor(f"xb{p}", [128, DM], BF16))
            for p in range(2)]
    jv_sb = poolF[0:1, :]
    jv_b = poolB[0:1, :]
    jsq_b = poolB[32:33, :]
    onesrow_b = poolB[64:65, :]

    ld_jv = P.emit("y", lambda y: y.dma_start(jv_sb, jv_d[:, :]), sem="y_jv")
    ld_eye = P.emit("y", lambda y: y.dma_start(eye_f[:, :], eye_d[:, :]), sem="y_eye")
    gw_small = 0
    for dst, src in [
        (eye_b[:, :], eye_d[:, :]), (bq_sb[:, :], bq_d[:, :]),
        (bk_sb[:, :], bk_d[:, :]), (bo_sb[:, :], bo_d[:, :]),
        (bfq_sb[:, :], bfq_d[:, :]), (bfg_sb[:, :], bfg_d[:, :]),
        (bv_sb[:, :], bv_d[:, :]), (wg_b[:, :], wg_d[:, :]),
        (ltri_b[:, :], ltri_d[:, :]), (ecap_sb[:, :], ecap_d[:, :]),
    ]:
        gw_small = P.emit("g", lambda g, d=dst, s=src: g.dma_start(d, s), sem="g_sm")

    def memsets(v):
        v.memset(pre2[:, 0:98:2], -1.0)
        v.memset(pre2[:, 1:98:2], 0.0)
        v.memset(cpool[:, 0:1], 1e-5)
        v.memset(ones1_b[:, :], 1.0)
        v.memset(ones128_b[:, :], 1.0)
        return v.memset(onesrow_b, 1.0)
    v_init = P.emit("v", memsets)
    P.emit("g", lambda g: g.dma_start(
        bass.AP(mdd, 0, [[64, 128], [8192, 49], [1, 2]]),
        pre2[:, :].rearrange("p (j e) -> p j e", e=2)),
        waits=[("v", v_init)], sem="g_pf")
    P.emit("g", lambda g: g.iota(tokf[:, :], [[128, 16]], base=1,
                                 channel_multiplier=1,
                                 allow_small_or_imprecise_dtypes=True),
           sem="g_io")
    gw_io = P.cnt["g_io"]
    P.emit("g", lambda g: g.load_library(MLP_LIB), inc=False)
    gw_pf = P.cnt["g_pf"]
    v_jsq = P.emit("v", lambda v: v.tensor_tensor(jsq_b, jv_sb, jv_sb,
                                                  ALU.mult),
                   waits=[("y_jv", ld_jv)])
    v_jb = P.emit("v", lambda v: v.tensor_copy(jv_b, jv_sb),
                  waits=[("v", v_jsq)])
    gw_j = 0
    for ka in (k_augA, k_augB):
        P.emit("g", lambda g, ka=ka: g.dma_start(ka[64:65, :], jsq_b),
               waits=[("v", v_jb), ("v", v_init)], sem="g_j")
        P.emit("g", lambda g, ka=ka: g.dma_start(ka[65:66, :], jv_b),
               sem="g_j")
        gw_j = P.emit("g", lambda g, ka=ka: g.dma_start(ka[66:67, :],
                                                        onesrow_b), sem="g_j")

    x_ld, x_cv, x_tr, x_ev = {}, {}, {}, {}
    for t in range(NXT):
        par = t % 2
        x_ld[t] = P.emit("y", lambda y, t=t, par=par: y.dma_start(
            stage_a[par][:, :], xr[t * 128:(t + 1) * 128, :]),
            waits=[("v", x_cv.get(t - 2, 0))], sem=f"y_x{par}")
        x_cv[t] = P.emit("v", lambda v, par=par: v.tensor_copy(
            xb_t[par][:, :], stage_a[par][:, :]),
            waits=[(f"y_x{par}", x_ld[t]), ("t", x_tr.get(t - 2, 0)),
                   ("v", v_init)])

        def trf(t_, par=par):
            for dc in range(NDC):
                ii = t_.transpose(ps_tr[par][:, dc * 128:(dc + 1) * 128],
                                  xb_t[par][:, dc * 128:(dc + 1) * 128],
                                  eye_b[:, :])
            return ii
        x_tr[t] = P.emit("t", trf, waits=[("v", x_cv[t]),
                                          ("g_sm", gw_small),
                                          ("v", x_ev.get(t - 2, 0))])

        def evf(v, t=t, par=par):
            for dc in range(NDC):
                ii = v.tensor_copy(x_fm[dc][:, t * 128:(t + 1) * 128],
                                   ps_tr[par][:, dc * 128:(dc + 1) * 128])
            return ii
        x_ev[t] = P.emit("v", evf, waits=[("t", x_tr[t])])
    x_done = x_ev[NXT - 1]
    xtr_last = x_tr[NXT - 1]

    sc_p0.close()
    sc_ptr.close()
    ps_d = nc.alloc_psum_tensor("ps_d", [128, 2 * QT], F32)

    sc_c3 = contextlib.ExitStack()
    C3 = sc_c3.enter_context(nc.sbuf_tensor("C3", [80, TPC], BF16))
    sc_w = contextlib.ExitStack()
    w_big1 = sc_w.enter_context(nc.sbuf_tensor("w_big1", [128, 8 * DM], BF16))
    w_big2 = w_big1

    gw_wq = P.emit("g", lambda g: g.dma_start(
        w_big1[:, :].rearrange("p (c n) -> p c n", c=8), chunked(wq_d[:, :])),
        waits=[("t", xtr_last), ("v", x_done), ("g_j", gw_j)], sem="g_wb")

    mq_mm, mq_ev = {}, {}
    for qc in range(NQT):
        for dc in range(NDC):
            i = qc * NDC + dc
            par = i % 2

            def mmf(t_, qc=qc, dc=dc, par=par):
                for di in range(NDC):
                    ii = t_.matmul(ps_s[par][:, 0:QT], wsl(w_big1, di, dc),
                                   x_fm[di][:, qc * QT:(qc + 1) * QT],
                                   start=(di == 0), stop=(di == NDC - 1))
                return ii
            mq_mm[i] = P.emit("t", mmf, waits=[("v", x_done), ("g_wb", gw_wq),
                                               ("v", mq_ev.get(i - 2, 0))])
            mq_ev[i] = P.emit("v", lambda v, qc=qc, dc=dc, par=par:
                              v.tensor_scalar(mq[dc][:, qc * QT:(qc + 1) * QT],
                                              ps_s[par][:, 0:QT],
                                              bq_sb[:, dc:dc + 1], None,
                                              ALU.add),
                              waits=[("t", mq_mm[i])])
    mq_done = mq_ev[NQT * NDC - 1]

    def xmeanf(v):
        for dc in range(NDC):
            ii = v.reduce_sum(xm_sb[:, dc:dc + 1], x_fm[dc][:, :], axis=AX.X)
        return ii
    v_xm = P.emit("v", xmeanf, waits=[("v", x_done)])
    v_xmb = P.emit("v", lambda v: v.tensor_scalar(
        xm_b[:, :], xm_sb[:, :], 1.0 / S, None, ALU.mult), waits=[("v", v_xm)])

    def glof(t_):
        for dc in range(NDC):
            for di in range(NDC):
                ii = t_.matmul(ps_c[0][:, dc:dc + 1], wsl(w_big1, di, dc),
                               xm_b[:, di:di + 1], start=(di == 0),
                               stop=(di == NDC - 1))
        return ii
    t_glo = P.emit("t", glof, waits=[("v", v_xmb), ("g_wb", gw_wq)])
    v_glo = P.emit("v", lambda v: v.tensor_tensor(
        glo_sb[:, :], ps_c[0][:, 0:8], bq_sb[:, :], ALU.add),
        waits=[("t", t_glo)])
    v_glob = P.emit("v", lambda v: v.tensor_copy(glo_b[:, :], glo_sb[:, :]),
                    waits=[("v", v_glo)])

    def gff(t_):
        for dc in range(NDC):
            for di in range(NDC):
                ii = t_.matmul(ps_c[1][:, dc:dc + 1], wsl(w_big2, di, dc),
                               glo_b[:, di:di + 1], start=(di == 0),
                               stop=(di == NDC - 1))
        return ii
    gw_wfg = P.emit("g", lambda g: g.dma_start(
        w_big1[:, :].rearrange("p (c n) -> p c n", c=8), chunked(wfg_d[:, :])),
        waits=[("t", t_glo), ("t", mq_mm[NQT * NDC - 1])], sem="g_wb")
    t_gf = P.emit("t", gff, waits=[("v", v_glob), ("g_wb", gw_wfg)])
    v_cb1 = P.emit("v", lambda v: v.tensor_tensor(
        cbias[:, :], ps_c[1][:, 0:8], bfq_sb[:, :], ALU.add),
        waits=[("t", t_gf)])
    v_cb = P.emit("v", lambda v: v.tensor_tensor(
        cbias[:, :], cbias[:, :], bfg_sb[:, :], ALU.add), waits=[("v", v_cb1)])

    gw_wfq = P.emit("g", lambda g: g.dma_start(
        w_big2[:, :].rearrange("p (c n) -> p c n", c=8), chunked(wfq_d[:, :])),
        waits=[("t", t_gf)], sem="g_wb")

    sc_p2 = contextlib.ExitStack()
    upz_b = sc_p2.enter_context(nc.sbuf_tensor("upz_b", [128, 512], BF16))
    gw_upz = P.emit("g", lambda g: g.dma_start(upz_b[:, :], upz_d[:, :]),
                    waits=[("t", xtr_last), ("v", x_done), ("g_j", gw_j)],
                    sem="g_upz")
    spz = sc_p2.enter_context(nc.sbuf_tensor("spz", [128, 4096], F32))
    Sp, Sz = spz[0:16, 0:TPC], spz[32:48, 0:TPC]
    cA, cB = spz[64:80, 0:TPC], spz[64:80, TPC:2 * TPC]
    ctp = sc_p2.enter_context(nc.sbuf_tensor("ctp", [128, 1024], BF16))
    c_t_sb = [ctp[:, 0:QT], ctp[:, QT:2 * QT]]

    c_mm, c_tanh, pz_mm, sig = {}, {}, {}, {}
    for qc in range(NQT):
        for dc in range(NDC):
            i = qc * NDC + dc
            par = i % 2

            def cmf(t_, qc=qc, dc=dc, par=par):
                for di in range(NDC):
                    ii = t_.matmul(ps_s[par][:, 0:QT], wsl(w_big2, di, dc),
                                   mq[di][:, qc * QT:(qc + 1) * QT],
                                   start=(di == 0), stop=(di == NDC - 1))
                return ii
            c_mm[i] = P.emit("t", cmf,
                             waits=[("v", v_cb), ("g_wb", gw_wfq), ("v", mq_done),
                                    ("s", c_tanh.get(i - 2, 0))])
            c_tanh[i] = P.emit("s", lambda s, dc=dc, par=par: s.activation(
                c_t_sb[par], ps_s[par][:, 0:QT], AF.Tanh,
                bias=cbias[:, dc:dc + 1]), waits=[("t", c_mm[i])])
            pz_mm[i] = P.emit("t", lambda t_, dc=dc, par=par: t_.matmul(
                ps_c[0][0:64, 0:QT], upz_b[:, dc * 64:(dc + 1) * 64],
                c_t_sb[par], start=(dc == 0), stop=(dc == NDC - 1)),
                waits=[("s", c_tanh[i]), ("s", sig.get(qc - 1, 0)),
                       ("v", v_glo if i == 0 else 0),
                       ("g_upz", gw_upz if i == 0 else 0)])
        sig[qc] = P.emit("s", lambda s, qc=qc: (
            s.activation(Sp[:, qc * QT:(qc + 1) * QT], ps_c[0][0:16, 0:QT],
                         AF.Sigmoid),
            s.activation(Sz[:, qc * QT:(qc + 1) * QT], ps_c[0][32:48, 0:QT],
                         AF.Sigmoid))[-1],
            waits=[("t", pz_mm[qc * NDC + NDC - 1])])
    sig_done = sig[NQT - 1]
    pz_last = pz_mm[NQT * NDC - 1]

    steps = [
        lambda v: v.reciprocal(cA, Sz),
        lambda v: v.tensor_tensor(cA, cA, cA, ALU.mult),
        lambda v: v.tensor_scalar(cA, cA, -1.1920928955078125e-07,
                                  None, ALU.mult),
        lambda v: v.tensor_scalar(cB, Sp, float(S), None, ALU.mult),
        lambda v: v.tensor_copy(C3[0:16, 0:TPC], cA),
        lambda v: v.tensor_tensor(cA, cA, cB, ALU.mult),
        lambda v: v.tensor_scalar(C3[32:48, 0:TPC], cA, -2.0, None,
                                  ALU.mult),
        lambda v: v.tensor_tensor(cA, cA, cB, ALU.mult),
        lambda v: v.tensor_scalar(C3[64:80, 0:TPC], cA, -4.0, None,
                                  ALU.add),
    ]
    vprev = P.emit("v", steps[0], waits=[("s", sig_done)])
    for f in steps[1:]:
        vprev = P.emit("v", f, waits=[("v", vprev)])
    coeff_done = vprev

    sc_p2.close()
    sc_w.close()

    sc_att2 = contextlib.ExitStack()
    v_augs = [sc_att2.enter_context(nc.sbuf_tensor(f"v_aug{i}",
                                                   [128, NKC * 130], BF16))
              for i in range(2)]
    kv_w = [sc_att2.enter_context(nc.sbuf_tensor(f"kv{p}", [128, 1024], BF16))
            for p in range(2)]
    wv2 = sc_att2.enter_context(nc.sbuf_tensor("wv2", [128, 2048], BF16))
    DEP = 6
    e_t = [sc_att2.enter_context(nc.sbuf_tensor(f"e_t{p}", [128, 2 * QT],
                                                BF16)) for p in range(DEP)]
    rhs_aug = [sc_att2.enter_context(nc.sbuf_tensor(f"rhs{p}", [128, QT],
                                                    BF16)) for p in range(4)]
    rbp = sc_att2.enter_context(nc.sbuf_tensor("rbp", [128, QT], F32))
    rb_sb = [rbp[0:64, :], rbp[64:128, :]]
    bvb2 = sc_att2.enter_context(nc.sbuf_tensor("bvb2", [128, 256], F32))

    def vones(v):
        for va_t in v_augs:
            va = va_t[:, :].rearrange("p (t c) -> p t c", c=130)
            v.memset(va[:, :, 64:65], 1.0)
            ii = v.memset(va[:, :, 129:130], 1.0)
        return ii
    v_vones = P.emit("v", vones, waits=[("t", pz_last), ("v", coeff_done)])

    SB3 = [ps_s[0], ps_s[1], ps_d]

    AT = {}
    exp_hist = []
    ctx_hist = []
    pair_t_end = {}
    for p in range(NE):
        wpar = p % 2
        cols = slice(p * 128, (p + 1) * 128)
        kv_gw = P.emit("g", lambda g, wpar=wpar, cols=cols: g.dma_start(
            kv_w[wpar][:, :].rearrange("p (c n) -> p c n", c=8),
            chunked(wk_d[:, cols])),
            waits=[("t", pair_t_end.get(p - 2, pz_last)), ("g_j", gw_j),
                   ("v", coeff_done if p == 0 else 0),
                   ("s", sig_done if p == 0 else 0)], sem=f"g_kv{wpar}")
        if p % 2 == 0:
            cols2 = slice(p * 128, (p + 2) * 128)
            gw_wv = P.emit("g", lambda g, cols2=cols2: g.dma_start(
                wv2[:, :].rearrange("p (c n) -> p c n", c=8),
                chunked(wv_d[:, cols2])),
                waits=[("t", AT.get((p - 2, "v_done"), pz_last))], sem="g_wv")
            bv_dma = P.emit("y", lambda y, p=p: y.dma_start(
                bvb2[:, :], bass.AP(bvr_d, p * 128, [[0, 128], [1, 256]])),
                waits=[("v", AT.get((p - 2, "bv_used"), 0)),
                       ("v", v_init)], sem="y_bv")

        kev = {}
        for kt in range(8):
            par = kt % 2

            def kmf(t_, kt=kt, par=par, wpar=wpar):
                for di in range(NDC):
                    ii = t_.matmul(ps_s[par][:, 0:QT],
                                   kv_w[wpar][:, di * 128:(di + 1) * 128],
                                   x_fm[di][:, kt * QT:(kt + 1) * QT],
                                   start=(di == 0), stop=(di == NDC - 1))
                return ii
            w = [(f"g_kv{wpar}", kv_gw), ("v", kev.get(kt - 2, 0))]
            if kt == 0:
                w += [("s", exp_hist[-1] if exp_hist else sig_done),
                      ("v", AT.get((p - 1, NQT - 1, "ctxe"),
                                   max(coeff_done, mq_done)))]
            tk = P.emit("t", kmf, waits=w)
            kev[kt] = P.emit("v", lambda v, kt=kt, par=par, p=p: (
                v.tensor_scalar(k_augA[0:64, kt * QT:(kt + 1) * QT],
                                ps_s[par][0:64, 0:QT], bk_sb[0:64, p:p + 1],
                                0.125, ALU.add, ALU.mult),
                v.tensor_scalar(k_augB[0:64, kt * QT:(kt + 1) * QT],
                                ps_s[par][64:128, 0:QT], bk_sb[64:128, p:p + 1],
                                0.125, ALU.add, ALU.mult))[-1],
                waits=[("t", tk),
                       ("t", AT.get((p - 1, NQT - 1, "last_sc"), 0)
                        if kt == 0 else 0)])

        if p % 2 == 0:
            vev = {}
            for tt in range(NKC):
                par = tt % 2

                def vmf(t_, tt=tt, par=par):
                    for di in range(NDC):
                        ii = t_.matmul(ps_s[par][:, QT:QT + 256],
                                       x_fm[di][:, tt * 128:(tt + 1) * 128],
                                       wv2[:, di * 256:(di + 1) * 256],
                                       start=(di == 0), stop=(di == NDC - 1))
                    return ii
                w = [("v", vev.get(tt - 2, 0)), ("g_wv", gw_wv)]
                if tt == 0:
                    w += [("t", AT.get((p - 1, NQT - 1, NKC - 1, "ctx"), 0))]
                tv = P.emit("t", vmf, waits=w)
                base = tt * 130

                def vevf(v, base=base, par=par):
                    v.tensor_tensor(v_augs[0][:, base:base + 64],
                                    ps_s[par][:, QT:QT + 64], bvb2[:, 0:64],
                                    ALU.add)
                    v.tensor_tensor(v_augs[0][:, base + 65:base + 129],
                                    ps_s[par][:, QT + 64:QT + 128],
                                    bvb2[:, 64:128], ALU.add)
                    v.tensor_tensor(v_augs[1][:, base:base + 64],
                                    ps_s[par][:, QT + 128:QT + 192],
                                    bvb2[:, 128:192], ALU.add)
                    return v.tensor_tensor(v_augs[1][:, base + 65:base + 129],
                                           ps_s[par][:, QT + 192:QT + 256],
                                           bvb2[:, 192:256], ALU.add)
                vev[tt] = P.emit("v", vevf,
                                 waits=[("t", tv), ("y_bv", bv_dma),
                                        ("v", v_vones if tt == 0 and p == 0
                                         else 0)])
            AT[(p, "v_done")] = P.cnt["t"]
            AT[(p, "bv_used")] = vev[NKC - 1]
            kv_done_v = max(vev[NKC - 1], kev[7])
        else:
            kv_done_v = kev[7]
        v_aug = v_augs[p % 2]

        def emit_rhs(pp, qc):
            for hi in range(2):
                bi = 2 * (qc % 2) + hi
                h = 2 * pp + hi
                guard = AT.get((pp, qc - 2, "last_sc"),
                               AT.get((pp - 1, NQT - 1, "last_sc"), pz_last))
                AT[(pp, qc, hi, "q")] = P.emit(
                    "v", lambda v, bi=bi, hi=hi, pp=pp, qc=qc: v.tensor_copy(
                        rhs_aug[bi][0:64, :],
                        mq[pp][hi * 64:(hi + 1) * 64, qc * QT:(qc + 1) * QT]),
                    waits=[("t", guard)])
                for ri in range(3):
                    AT[(pp, qc, hi, "c3")] = P.emit(
                        "g", lambda g, bi=bi, h=h, qc=qc, ri=ri: g.dma_start(
                            rhs_aug[bi][64 + ri:65 + ri, 0:QT],
                            C3[h + 32 * ri:h + 32 * ri + 1,
                               qc * QT:(qc + 1) * QT]),
                        waits=[("v", AT[(pp, qc, hi, "q")])]
                        if ri == 0 else [], sem=f"g_c3{qc % 2}")

        if p == 0:
            emit_rhs(p, 0)
        emit_rhs(p, 1)
        for qc in range(NQT):
            def emit_ctx(kc):
                g = len(ctx_hist)
                assert g % NKC == kc or True
                gb = (qc * NKC + kc)

                def ctf(t_, kc=kc, va=v_aug, eb=e_t[len(ctx_hist) % DEP]):
                    base = kc * 130
                    t_.matmul(ps_c[0][0:65, :], va[:, base:base + 65],
                              eb[:, 0:QT], start=(kc == 0),
                              stop=(kc == NKC - 1))
                    return t_.matmul(ps_c[1][0:65, :],
                                     va[:, base + 65:base + 130],
                                     eb[:, QT:2 * QT],
                                     start=(kc == 0), stop=(kc == NKC - 1))
                wct = [("s", AT[(p, qc, kc, "exp")])]
                if kc == 0:
                    wct += [("v", AT.get((p, qc - 1, "ctxe"),
                                         AT.get((p - 1, NQT - 1, "ctxe"), 0))),
                            ("v", kv_done_v)]
                AT[(p, qc, kc, "ctx")] = P.emit("t", ctf, waits=wct)
                ctx_hist.append(AT[(p, qc, kc, "ctx")])

            for kc in range(NKC):
                if kc >= DEP:
                    emit_ctx(kc - DEP)
                g = len(exp_hist)
                b3 = SB3[g % 3]
                wsc = [("s", exp_hist[g - 3] if g >= 3 else 0),
                       (f"g_c3{qc % 2}", AT[(p, qc, 1, "c3")]),
                       ("v", kv_done_v)]
                if kc == 0:
                    wsc += [("g_j", gw_j if p == 0 and qc == 0 else 0)]

                def scf(t_, kc=kc, b3=b3, biA=2 * (qc % 2),
                        biB=2 * (qc % 2) + 1):
                    t_.matmul(b3[:, 0:QT],
                              k_augA[0:67, kc * 128:(kc + 1) * 128],
                              rhs_aug[biA][0:67, :], start=True, stop=True)
                    return t_.matmul(b3[:, QT:2 * QT],
                                     k_augB[0:67, kc * 128:(kc + 1) * 128],
                                     rhs_aug[biB][0:67, :], start=True,
                                     stop=True)
                AT[(p, qc, kc, "sc")] = P.emit("t", scf, waits=wsc)
                AT[(p, qc, kc, "exp")] = P.emit(
                    "s", lambda s, b3=b3, eb=e_t[g % DEP]: s.activation(
                        eb[:, :], b3[:, :], AF.Exp),
                    waits=[("t", AT[(p, qc, kc, "sc")]),
                           ("t", ctx_hist[g - DEP] if g >= DEP else 0)])
                exp_hist.append(AT[(p, qc, kc, "exp")])
                if kc == 2 and qc >= 1:
                    if qc + 1 < NQT:
                        emit_rhs(p, qc + 1)
                    elif p + 1 < NE:
                        emit_rhs(p + 1, 0)
            for kc2 in range(NKC - DEP, NKC):
                emit_ctx(kc2)
            AT[(p, qc, "last_sc")] = AT[(p, qc, NKC - 1, "sc")]
            AT[(p, qc, "rec")] = P.emit("v", lambda v: (
                v.tensor_copy(recr_f[0], ps_c[0][64:65, :]),
                v.tensor_copy(recr_f[1], ps_c[1][64:65, :]))[-1],
                waits=[("t", AT[(p, qc, NKC - 1, "ctx")]),
                       ("y_rb", AT.get((p, qc - 1, "rb"),
                                       AT.get((p - 1, NQT - 1, "rb"), 0)))])
            wr_rbd = P.emit("y", lambda y: y.dma_start(
                rbd[:, :], rowsF[0:1, 0:2 * QT]),
                waits=[("v", AT[(p, qc, "rec")])], sem="y_rb")
            AT[(p, qc, "rb")] = P.emit("y", lambda y: y.dma_start(
                rbp[:, :], bass.AP(rbd, 0, [[512, 2], [0, 64], [1, QT]])),
                waits=[("y_rb", wr_rbd),
                       ("v", AT.get((p, qc - 1, "ctxe"),
                                    AT.get((p - 1, NQT - 1, "ctxe"), 0)))],
                sem="y_rb")
            AT[(p, qc, "rcp")] = P.emit("v", lambda v: (
                v.reciprocal(rb_sb[0], rb_sb[0]),
                v.reciprocal(rb_sb[1], rb_sb[1]))[-1],
                waits=[("y_rb", AT[(p, qc, "rb")])])
            AT[(p, qc, "ctxe")] = P.emit("v", lambda v, p=p, qc=qc: (
                v.tensor_tensor(ctx_fm[p][0:64, qc * QT:(qc + 1) * QT],
                                ps_c[0][0:64, :], rb_sb[0], ALU.mult),
                v.tensor_tensor(ctx_fm[p][64:128, qc * QT:(qc + 1) * QT],
                                ps_c[1][0:64, :], rb_sb[1], ALU.mult))[-1],
                waits=[("v", AT[(p, qc, "rcp")])])
        pair_t_end[p] = P.cnt["t"]
    att_v = AT[(NE - 1, NQT - 1, "ctxe")]
    att_rbs = AT[(NE - 1, NQT - 1, NKC - 1, "exp")]
    att_t = P.cnt["t"]

    sc_att2.close()
    sc_c3.close()
    sc_kaug.close()
    sc_p4 = contextlib.ExitStack()
    wo_b = sc_p4.enter_context(nc.sbuf_tensor("wo_b", [128, 8 * DM], BF16))
    zbuf_a = sc_p4.enter_context(nc.sbuf_tensor("zbuf_a", [128, 8 * QT], F32))
    mb_a = sc_p4.enter_context(nc.sbuf_tensor("mb_a", [128, QT], F32))
    ivb_a = sc_p4.enter_context(nc.sbuf_tensor("ivb_a", [128, QT], F32))
    rows_a = sc_p4.enter_context(nc.sbuf_tensor("rows_a", [1, 2048], F32))
    zbp_a = [sc_p4.enter_context(nc.sbuf_tensor(f"zbp_a{p}", [128, QT], BF16))
             for p in range(2)]
    zsp_a = [sc_p4.enter_context(nc.sbuf_tensor(f"zsp_a{p}", [128, QT], BF16))
             for p in range(2)]
    gsig = sc_p4.enter_context(nc.sbuf_tensor("gsig", [128, 32], F32))
    gt = sc_p4.enter_context(nc.sbuf_tensor("gt", [128, 160], F32))
    wtok = sc_p4.enter_context(nc.sbuf_tensor("wtok", [128, 32], F32))
    ystg = [sc_p4.enter_context(nc.sbuf_tensor(f"ystg{p}", [128, DM], F32))
            for p in range(2)]
    ystg_b = [sc_p4.enter_context(nc.sbuf_tensor(f"ystb{p}", [128, DM], BF16))
              for p in range(2)]

    gw_wo = P.emit("g", lambda g: g.dma_start(
        wo_b[:, :].rearrange("p (c n) -> p c n", c=8), chunked(wo_d[:, :])),
        waits=[("t", att_t), ("s", att_rbs), ("v", att_v)], sem="g_wb")

    def ln_phase(qc, mode, y_dst, st_, zbuf, mb_sb, invb_sb, rows4,
                 zb_t, zsq_t):
        L = {}
        mrow, srow = rows4[0:1, 0:512], rows4[0:1, 512:1024]
        vrow, vrow2 = rows4[0:1, 1024:1536], rows4[0:1, 1536:2048]
        zev = st_.setdefault("zev", {})
        for dc in range(NDC):
            gi = qc * NDC + dc
            par = gi % 2
            if mode == "attn":
                wmm = [("v", att_v), ("g_wb", gw_wo),
                       ("v", zev.get(gi - 2, 0)),
                       ("s", att_rbs if gi < 2 else 0),
                       ("s", st_.get("prev_bcs", 0) if dc < 2 else 0)]

                def mmf(t_, qc=qc, dc=dc, par=par):
                    for di in range(NDC):
                        ii = t_.matmul(ps_s[par][:, 0:QT], wsl(wo_b, di, dc),
                                       ctx_fm[di][:, qc * QT:(qc + 1) * QT],
                                       start=(di == 0), stop=(di == NDC - 1))
                    return ii
                L[(dc, "mm")] = P.emit("t", mmf, waits=wmm)
                zev[gi] = L[(dc, "z")] = P.emit(
                    "v", lambda v, qc=qc, dc=dc, par=par:
                    v.scalar_tensor_tensor(
                        zbuf[:, dc * QT:(dc + 1) * QT], ps_s[par][:, 0:QT],
                        bo_sb[:, dc:dc + 1],
                        x_fm[dc][:, qc * QT:(qc + 1) * QT], ALU.add, ALU.add),
                    waits=[("t", L[(dc, "mm")]),
                           ("t", st_.get("prev_ttr", 0) if dc < 2 else 0)])
            else:
                L[(dc, "z")] = P.emit(
                    "v", lambda v, qc=qc, dc=dc: v.scalar_tensor_tensor(
                        zbuf[:, dc * QT:(dc + 1) * QT],
                        st_["routed"][dc][:, qc * QT:(qc + 1) * QT], 1.0 / 3.0,
                        y1[dc][:, qc * QT:(qc + 1) * QT], ALU.mult, ALU.add),
                    waits=[("v", st_.get("src_v", 0)),
                           ("t", st_.get("prev_ttr", 0))])
            zg = (L.get((dc - 2, "sums"), st_.get("prev_sums_last", 0)
                        if dc < 2 else 0))
            L[(dc, "zb")] = P.emit("v", lambda v, dc=dc: v.tensor_copy(
                zb_t[dc % 2][:, :], zbuf[:, dc * QT:(dc + 1) * QT]),
                waits=[("v", L[(dc, "z")]), ("t", zg)])
            L[(dc, "zsq")] = P.emit("s", lambda s, dc=dc: s.activation(
                zsq_t[dc % 2][:, :], zbuf[:, dc * QT:(dc + 1) * QT],
                AF.Square), waits=[("v", L[(dc, "z")]), ("t", zg)])

            def emit_sums(d2):
                wsum = [("v", L[(d2, "zb")]), ("s", L[(d2, "zsq")])]
                if d2 == 0:
                    wsum += [("s", st_.get("psc_guard_s", 0)),
                             ("v", st_.get("psc_guard_v", 0))]
                L[(d2, "sums")] = P.emit("t", lambda t_, d2=d2: (
                    t_.matmul(ps_c[0][0:1, 0:QT], ones128_b[:, :],
                              zb_t[d2 % 2][:, :], start=(d2 == 0),
                              stop=(d2 == NDC - 1)),
                    t_.matmul(ps_c[1][0:1, 0:QT], ones128_b[:, :],
                              zsq_t[d2 % 2][:, :], start=(d2 == 0),
                              stop=(d2 == NDC - 1)))[-1], waits=wsum)

            if dc >= 1:
                emit_sums(dc - 1)
            if dc == NDC - 1:
                emit_sums(dc)
        r0 = P.emit("v", lambda v: v.tensor_scalar(
            mrow, ps_c[0][0:1, 0:QT], 1.0 / DM, None, ALU.mult),
            waits=[("t", L[(NDC - 1, "sums")])])
        r1 = P.emit("v", lambda v: v.tensor_scalar(
            vrow, ps_c[1][0:1, 0:QT], 1.0 / DM, None, ALU.mult),
            waits=[("v", r0)])
        r2 = P.emit("v", lambda v: v.tensor_tensor(vrow2, mrow, mrow,
                                                   ALU.mult), waits=[("v", r1)])
        r3 = P.emit("v", lambda v: v.tensor_tensor(vrow, vrow, vrow2,
                                                   ALU.subtract),
                    waits=[("v", r2)])
        r4 = P.emit("s", lambda s: s.activation(srow, vrow, AF.Sqrt,
                                                bias=1e-5), waits=[("v", r3)])
        r5 = P.emit("v", lambda v: v.reciprocal(vrow2, srow),
                    waits=[("s", r4)])
        r6 = P.emit("v", lambda v: (v.tensor_copy(mrow_b, mrow),
                                    v.tensor_copy(irow_b, vrow2))[-1],
                    waits=[("v", r5)])
        wbc = [("v", r6), ("v", L[(NDC - 1, "z")]),
               ("s", st_.get("pss_guard_s", 0)),
               ("v", st_.get("pss_guard_v", 0))]
        bct = P.emit("t", lambda t_: (
            t_.matmul(ps_s[0][:, 0:QT], ones1_b[:, :], mrow_b, start=True,
                      stop=True),
            t_.matmul(ps_s[1][:, 0:QT], ones1_b[:, :], irow_b, start=True,
                      stop=True))[-1], waits=wbc)
        bcs = P.emit("s", lambda s: (s.copy(mb_sb[:, :], ps_s[0][:, 0:QT]),
                                     s.copy(invb_sb[:, :],
                                            ps_s[1][:, 0:QT]))[-1],
                     waits=[("t", bct)])
        subv = P.emit("v", lambda v: [v.tensor_tensor(
            zbuf[:, dc * QT:(dc + 1) * QT], zbuf[:, dc * QT:(dc + 1) * QT],
            mb_sb[:, :], ALU.subtract) for dc in range(NDC)][-1],
            waits=[("s", bcs)])
        finz = P.emit("v", lambda v: [v.tensor_tensor(
            zbuf[:, dc * QT:(dc + 1) * QT], zbuf[:, dc * QT:(dc + 1) * QT],
            invb_sb[:, :], ALU.mult) for dc in range(NDC)][-1],
            waits=[("v", subv)])
        if y_dst is not None:
            fin = P.emit("v", lambda v, qc=qc: [v.tensor_copy(
                y_dst[dc][:, qc * QT:(qc + 1) * QT],
                zbuf[:, dc * QT:(dc + 1) * QT]) for dc in range(NDC)][-1],
                waits=[("v", finz)])
        else:
            fin = finz
        L["finz"] = finz
        st_["prev_sums_last"] = L[(NDC - 1, "sums")]
        st_["prev_bcs"] = bcs
        st_["pss_guard_s"] = bcs
        st_["psc_guard_v"] = r1
        L["fin"], L["bcs"], L["r1"] = fin, bcs, r1
        return L

    st1 = {"psc_guard_s": sig_done}
    gate_ws = {}
    YS = {}
    ln1_zlast = 0
    ln1_mmlast = 0
    for qc in range(NQT):
        L1 = ln_phase(qc, "attn", y1, st1, zbuf_a, mb_a, ivb_a, rows_a,
                      zbp_a, zsp_a)
        ln1_zlast = L1[(NDC - 1, "z")]
        ln1_mmlast = L1[(NDC - 1, "mm")]
        gmt = {}
        gst = {}
        for tt in range(4):
            tok = qc * QT + tt * 128

            def gmf(t_, qc=qc, tt=tt, tok=tok):
                for di in range(NDC):
                    ii = t_.matmul(ps_c[0][:, 8 * tt:8 * tt + 8],
                                   y1[di][:, tok:tok + 128],
                                   wg_b[:, di * 8:(di + 1) * 8],
                                   start=(di == 0), stop=(di == NDC - 1))
                return ii
            gmt[tt] = P.emit("t", gmf, waits=[("v", L1["fin"]),
                                              ("s", gate_ws.get((qc - 1, 3),
                                                                0))])
        for tt in range(4):
            gst[tt] = P.emit("s", lambda s, tt=tt: s.activation(
                gsig[:, 8 * tt:8 * tt + 8], ps_c[0][:, 8 * tt:8 * tt + 8],
                AF.Sigmoid), waits=[("t", gmt[tt])])
        st1["psc_guard_s"] = gst[3]

        def gchain(tt):
            gg = gsig[:, 8 * tt:8 * tt + 8]
            G = lambda a, b: gt[:, 40 * tt + a:40 * tt + b]
            wt = wtok[:, 8 * tt:8 * tt + 8]
            return [
                lambda v: v.reduce_max(G(0, 1), gg, axis=AX.X),
                lambda v: v.tensor_scalar(G(8, 16), gg, G(0, 1), None,
                                          ALU.is_equal),
                lambda v: v.scalar_tensor_tensor(G(16, 24), G(8, 16), -1e30,
                                                 gg, ALU.mult, ALU.add),
                lambda v: v.reduce_max(G(1, 2), G(16, 24), axis=AX.X),
                lambda v: v.tensor_scalar(G(24, 32), G(16, 24), G(1, 2), None,
                                          ALU.is_equal),
                lambda v: v.tensor_tensor(G(24, 32), G(24, 32), G(8, 16),
                                          ALU.add),
                lambda v: v.tensor_tensor(G(2, 3), G(0, 1), G(1, 2), ALU.add),
                lambda v: v.reciprocal(G(3, 4), G(2, 3)),
                lambda v: v.tensor_tensor(G(32, 40), gg, G(24, 32), ALU.mult),
                lambda v: v.tensor_scalar(wt, G(32, 40), G(3, 4), None,
                                          ALU.mult),
            ]
        chains = {tt: gchain(tt) for tt in range(4)}
        last_v = {tt: 0 for tt in range(4)}
        for step in range(10):
            for tt in range(4):
                w = [("s", gst[tt])] if step == 0 else [("v", last_v[tt])]
                last_v[tt] = P.emit("v", chains[tt][step], waits=w)
        for tt in range(4):
            k = qc * 4 + tt
            gate_ws[(qc, tt)] = P.emit("v", lambda v, tt=tt, k=k: (
                v.tensor_copy(selm[:, 8 * k:8 * k + 8], gt[:, 40 * tt + 24:
                                                           40 * tt + 32]),
                v.tensor_copy(wfull[:, 8 * k:8 * k + 8],
                              wtok[:, 8 * tt:8 * tt + 8]))[-1],
                waits=[("v", last_v[tt])])
        for tt in range(4):
            tok = qc * QT + tt * 128
            par = tt % 2

            def ytr(t_, tt=tt):
                for dc in range(NDC):
                    ii = t_.transpose(ps_d[:, dc * 128:(dc + 1) * 128],
                                      zbuf_a[:, dc * QT + tt * 128:
                                             dc * QT + (tt + 1) * 128],
                                      eye_f[:, :])
                return ii
            yt_t = P.emit("t", ytr, waits=[("v", L1["finz"]),
                                           ("v", YS.get(("cv", qc, tt - 1), 0)),
                                           ("y_eye", ld_eye)])
            YS[("tr", qc, tt)] = yt_t
            stg = YS.get(("st", qc, tt - 2), YS.get(("st", qc - 1, tt + 2), 0))
            stbg = YS.get(("stb", qc, tt - 2),
                          YS.get(("stb", qc - 1, tt + 2), 0))
            YS[("ev", qc, tt)] = P.emit(
                "v", lambda v, par=par: v.tensor_copy(ystg[par][:, :],
                                                      ps_d[:, :]),
                waits=[("t", yt_t), (f"y_y1{par}", stg),
                       (f"y_yb{par}", stbg)])
            YS[("cv", qc, tt)] = P.emit(
                "v", lambda v, par=par: v.tensor_copy(ystg_b[par][:, :],
                                                      ystg[par][:, :]),
                waits=[("v", YS[("ev", qc, tt)])])
            YS[("st", qc, tt)] = P.emit(
                "y", lambda y, par=par, tok=tok: y.dma_start(
                    y1f[tok:tok + 128, :], ystg[par][:, :]),
                waits=[("v", YS[("ev", qc, tt)])], sem=f"y_y1{par}")
            YS[("stb", qc, tt)] = P.emit(
                "y", lambda y, par=par, tok=tok: y.dma_start(
                    y1d[tok:tok + 128, :], ystg_b[par][:, :]),
                waits=[("v", YS[("cv", qc, tt)])], sem=f"y_yb{par}")
        st1["prev_ttr"] = YS[("tr", qc, 3)]
    y1d_done = [(f"y_y1{p}", YS[("st", NQT - 1, 2 + p)]) for p in range(2)] + \
        [(f"y_yb{p}", YS[("stb", NQT - 1, 2 + p)]) for p in range(2)]
    p4_v = P.cnt["v"]
    p4_s = P.cnt["s"]
    p4_t = P.cnt["t"]

    sc_rt = contextlib.ExitStack()
    msrc = sc_rt.enter_context(nc.sbuf_tensor("msrc", [128, 64], F32))
    swf = sc_rt.enter_context(nc.sbuf_tensor("swf", [128, 256], F32))
    swi16 = sc_rt.enter_context(nc.sbuf_tensor("swi16", [128, 256], I16))
    trn_sb = sc_rt.enter_context(nc.sbuf_tensor("trn_sb", [32, 128], F32))
    s2sb = sc_rt.enter_context(nc.sbuf_tensor("s2sb", [128, 32], F32))
    wsb = sc_rt.enter_context(nc.sbuf_tensor("wsb", [16, 256], F32))
    selm_done = gate_ws[(NQT - 1, 3)]
    pos_mm = P.emit("t", lambda t_: [t_.matmul(
        ps_d[:, 8 * k:8 * k + 8], ltri_b[:, :], selm[:, 8 * k:8 * k + 8],
        start=True, stop=True) for k in range(16)][-1],
        waits=[("v", selm_done), ("v", YS[("ev", NQT - 1, 3)])])
    tot_mm = P.emit("t", lambda t_: [t_.matmul(
        ps_s[0][0:1, 512 + 8 * k:520 + 8 * k], ones128_b[:, :],
        selm[:, 8 * k:8 * k + 8], start=True, stop=True)
        for k in range(16)][-1], waits=[("s", st1["pss_guard_s"])])
    v_tr = P.emit("v", lambda v: v.tensor_copy(prow[0:1, 0:128],
                                               ps_s[0][0:1, 512:640]),
                  waits=[("t", tot_mm)])
    yp = P.emit("y", lambda y: y.dma_start(t16d[:, :], prow[0:1, 0:128]),
                waits=[("v", v_tr)], sem="y_ps")
    yp = P.emit("y", lambda y: y.dma_start(
        t16[:, :], t16d[:, :].rearrange("o (a b) -> (o a) b", a=16)),
        waits=[("y_ps", yp)], sem="y_ps")
    v_t16 = P.emit("v", lambda v: v.tensor_copy(t16b[:, :], t16[:, :]),
                   waits=[("y_ps", yp)])
    b_mm = P.emit("t", lambda t_: t_.matmul(
        ps_c[0][0:16, 0:8], ltri_b[0:16, 0:16], t16b[:, :], start=True,
        stop=True), waits=[("v", v_t16), ("v", st1["psc_guard_v"]),
                           ("s", st1["psc_guard_s"])])
    v_b16 = P.emit("v", lambda v: v.tensor_copy(t16[:, :],
                                                ps_c[0][0:16, 0:8]),
                   waits=[("t", b_mm)])
    yp = P.emit("y", lambda y: y.dma_start(
        b16d[:, :].rearrange("o (a b) -> (o a) b", a=16), t16[:, :]),
        waits=[("v", v_b16)], sem="y_ps")
    yp = P.emit("y", lambda y: y.dma_start(
        base_b[:, :], bass.AP(b16d, 0, [[0, 128], [1, 128]])),
        waits=[("y_ps", yp)], sem="y_ps")
    v_pos = P.emit("v", lambda v: v.tensor_tensor(
        posg[:, :], ps_d[:, 0:128], base_b[:, :], ALU.add),
        waits=[("y_ps", yp), ("t", pos_mm)])

    def slot_steps(k):
        kb = 40 * (k % 4)
        T1 = slt[:, kb:kb + 8]
        TA = slt[:, kb + 8:kb + 16]
        SF = slt[:, kb + 16:kb + 24]
        MQ = slt[:, kb + 24:kb + 32]
        SF2 = slt[:, kb + 32:kb + 40]
        s1f = sl12[:, 2 * (k % 4):2 * (k % 4) + 1]
        s2f = sl12[:, 2 * (k % 4) + 1:2 * (k % 4) + 2]
        return [
            lambda v: v.tensor_tensor(T1, posg[:, 8 * k:8 * k + 8],
                                      ecap_sb[:, :], ALU.add),
            lambda v: v.tensor_scalar(MQ, posg[:, 8 * k:8 * k + 8],
                                      CAP - 0.5, None, ALU.is_gt),
            lambda v: v.scalar_tensor_tensor(T1, MQ, 1.0e5, T1, ALU.mult,
                                             ALU.add),
            lambda v: v.tensor_scalar(TA, T1, BIGF, None, ALU.add),
            lambda v: v.scalar_tensor_tensor(SF, selm[:, 8 * k:8 * k + 8],
                                             -BIGF, TA, ALU.mult, ALU.add),
            lambda v: v.tensor_reduce(s1f, SF, axis=AX.X, op=ALU.min),
            lambda v: v.tensor_scalar(MQ, SF, s1f, None, ALU.is_equal),
            lambda v: v.scalar_tensor_tensor(SF2, MQ, BIGF, SF, ALU.mult,
                                             ALU.add),
            lambda v: v.tensor_reduce(s2f, SF2, axis=AX.X, op=ALU.min),
            lambda v: v.tensor_tensor(TA, MQ, wfull[:, 8 * k:8 * k + 8],
                                      ALU.mult),
            lambda v: v.reduce_sum(wv1s[:, k:k + 1], TA, axis=AX.X),
            lambda v: v.tensor_scalar(MQ, SF2, s2f, None, ALU.is_equal),
            lambda v: v.tensor_tensor(TA, MQ, wfull[:, 8 * k:8 * k + 8],
                                      ALU.mult),
            lambda v: v.reduce_sum(wv2s[:, k:k + 1], TA, axis=AX.X),
            lambda v: (v.tensor_scalar(slotf[:, k:k + 1], s1f, 6144.0,
                                       None, ALU.min),
                       v.tensor_scalar(slotf[:, 16 + k:17 + k], s2f, 6144.0,
                                       None, ALU.min))[-1],
        ]
    for k4 in range(0, 16, 4):
        steps = {k: slot_steps(k) for k in range(k4, k4 + 4)}
        lastv = {k: 0 for k in steps}
        for st_i in range(15):
            for k in steps:
                w = [("v", v_pos)] if st_i == 0 else [("v", lastv[k])]
                lastv[k] = P.emit("v", steps[k][st_i], waits=w)
    slots_v = P.cnt["v"]
    mbld = P.emit("v", lambda v: (
        v.tensor_copy(msrc[:, 0:32:2], tokf[:, :]),
        v.tensor_copy(msrc[:, 1:32:2], wv1s[:, :]),
        v.tensor_copy(msrc[:, 32:64:2], tokf[:, :]),
        v.tensor_copy(msrc[:, 33:64:2], wv2s[:, :]))[-1],
        waits=[("v", slots_v), ("g_io", gw_io)])
    tr1 = P.emit("t", lambda t_: t_.transpose(
        ps_d[0:32, 512:640], slotf[:, :], eye_f[:, :]),
        waits=[("v", slots_v), ("v", YS[("ev", NQT - 1, 3)])])
    ev1 = P.emit("v", lambda v: v.tensor_copy(trn_sb[:, :],
                                              ps_d[0:32, 512:640]),
                 waits=[("t", tr1)])
    yb_ = P.emit("y", lambda y: y.dma_start(
        bass.AP(sld, 0, [[128, 32], [1, 128]]), trn_sb[:, :]),
        waits=[("v", ev1)], sem="y_ps")
    vsw = 0
    for r_ in range(2):
        yb_ = P.emit("y", lambda y, r_=r_: y.dma_start(
            s2sb[:, 16 * r_:16 * r_ + 16],
            bass.AP(sld, TPC * r_, [[16, 128], [1, 16]])),
            waits=[("y_ps", yb_)], sem="y_ps")
        tr2 = P.emit("t", lambda t_, r_=r_: t_.transpose(
            ps_d[0:16, 640 + 128 * r_:768 + 128 * r_],
            s2sb[:, 16 * r_:16 * r_ + 16], eye_f[:, :]),
            waits=[("y_ps", yb_)])
        ev2 = P.emit("v", lambda v, r_=r_: v.tensor_copy(
            wsb[:, 128 * r_:128 * r_ + 128],
            ps_d[0:16, 640 + 128 * r_:768 + 128 * r_]),
            waits=[("t", tr2)])
        yb_ = P.emit("y", lambda y, r_=r_: y.dma_start(
            bass.AP(sld, 2 * TPC + TPC * r_, [[128, 16], [1, 128]]),
            wsb[:, 128 * r_:128 * r_ + 128]),
            waits=[("v", ev2)], sem="y_ps")
        yb_ = P.emit("y", lambda y, r_=r_: y.dma_start(
            swf[:, 128 * r_:128 * r_ + 128],
            bass.AP(sld, 2 * TPC + TPC * r_, [[0, 8], [128, 16], [1, 128]])),
            waits=[("y_ps", yb_)], sem="y_ps")
        vsw = P.emit("v", lambda v, r_=r_: v.tensor_copy(
            swi16[:, 128 * r_:128 * r_ + 128],
            swf[:, 128 * r_:128 * r_ + 128]),
            waits=[("y_ps", yb_)])
    MD = {"done": None}

    def emit_md_tail():
        P.emit("g", lambda g: g.dma_scatter_add(
            mdd[:, 0:2], msrc[:, 0:32].rearrange("p (j e) -> p j e", e=2),
            swi16[:, 0:128], TPC, TPC, 2, elem_step=64, queue_num=3),
            waits=[("v", vsw), ("v", mbld), ("g_pf", gw_pf)], sem="g_md")
        P.emit("g", lambda g: g.dma_scatter_add(
            mdd[:, 0:2], msrc[:, 32:64].rearrange("p (j e) -> p j e", e=2),
            swi16[:, 128:256], TPC, TPC, 2, elem_step=64, queue_num=3),
            sem="g_md")
        P.emit("g", lambda g: g.dma_start(mdc[:, :], mdd[:, 0:2]),
               waits=[("g_md", P.cnt["g_md"])], sem="g_mc")
        for e_ in range(NE):
            P.emit("g", lambda g, e_=e_: g.dma_start(
                bass.AP(mdc2, e_ * 2 * CAP, [[1, 2 * CAP]]),
                bass.AP(mdc, e_ * CAP * 2, [[2, 16], [32, 48], [1, 2]])),
                waits=[("g_mc", P.cnt["g_mc"])], sem="g_mc")
        return P.cnt["g_mc"]
    sc_rt.close()

    sc_p4.close()
    sc_ctx.close()
    sc_x.close()

    sc_f = contextlib.ExitStack()
    we1s = [sc_f.enter_context(nc.sbuf_tensor(f"we1s{p}", [128, 8 * 512],
                                              BF16)) for p in range(2)]
    we3s = [sc_f.enter_context(nc.sbuf_tensor(f"we3s{p}", [128, 8 * 512],
                                              BF16)) for p in range(2)]
    we2h = [sc_f.enter_context(nc.sbuf_tensor(f"we2h{p}", [128, NFC * 512],
                                              BF16)) for p in range(2)]
    h_pool = sc_f.enter_context(nc.sbuf_tensor("h_pool",
                                               [128, (NFC - 4) * CAP], BF16))
    h_side = [sc_f.enter_context(nc.sbuf_tensor(f"h_sd{p}", [128, 4 * CAP],
                                                BF16)) for p in range(2)]
    pool_fm = [sc_f.enter_context(nc.sbuf_tensor(f"pfm{p}", [128, 8 * CAP],
                                                 BF16)) for p in range(2)]
    ost = [sc_f.enter_context(nc.sbuf_tensor(f"ost{p}", [128, 3 * DM], F32))
           for p in range(2)]
    sa_p = [sc_f.enter_context(nc.sbuf_tensor(f"sa_p{p}", [128, 512], BF16))
            for p in range(2)]
    h1_p = [sc_f.enter_context(nc.sbuf_tensor(f"h1_p{p}", [128, 512], F32))
            for p in range(2)]
    mdw = [sc_f.enter_context(nc.sbuf_tensor(f"mdw{p}", [128, 2 * CAP],
                                              F32)) for p in range(2)]
    gi_p = [sc_f.enter_context(nc.sbuf_tensor(f"gi_p{p}", [128, 96], F32))
            for p in range(2)]
    tmpf0 = sc_f.enter_context(nc.sbuf_tensor("tmpf0", [128, 96], F32))
    tmpf = [tmpf0, tmpf0]
    gi16 = [sc_f.enter_context(nc.sbuf_tensor(f"gi16{p}", [128, 48], I16))
            for p in range(2)]
    si16 = [sc_f.enter_context(nc.sbuf_tensor(f"si16{p}", [128, 48], I16))
            for p in range(2)]
    space_guard = [("v", ln1_zlast), ("t", ln1_mmlast), ("v", p4_v),
                   ("s", p4_s)] + y1d_done

    NG = CAP // 128
    SUBS = [(0, 512), (512, CAP)]

    def emit_exp_loads(e, md_done):
        par_e = e % 2
        prevld["gix"] = gixL = P.emit("g", lambda g, e=e, par_e=par_e:
                                      g.dma_start(
            gi_p[par_e][:, :],
            bass.AP(mdc2, e * 2 * CAP, [[0, 8], [96, 16], [1, 96]])),
            waits=[("g_mc", md_done), ("v", VC.get(e - 2, 0)),
                   ("g_ix", prevld["gix"])] +
                  (y1d_done if e == 0 else []), sem="g_ix")
        GIX[e] = gixL
        prevld["wbb"] = wbbL = P.emit("g", lambda g, e=e, par_e=par_e:
                                      g.dma_start(
            mdw[par_e][:, :],
            bass.AP(mdc, e * CAP * 2, [[0, 128], [1, 2 * CAP]])),
            waits=[("v", HT_WBB.get(e - 2, 0)), ("g_wb", prevld["wbb"]),
                   ("g_mc", md_done)] +
                  (y1d_done if e == 0 else []), sem="g_wb")
        WBB[e] = wbbL
        vc1 = P.emit("v", lambda v, par_e=par_e: (
            v.tensor_scalar(tmpf[par_e][:, 0:48], gi_p[par_e][:, 0:96:2],
                            0.0, None, ALU.max),
            v.tensor_scalar(tmpf[par_e][:, 48:96], gi_p[par_e][:, 0:96:2],
                            0.0, None, ALU.is_lt))[-1],
            waits=[("g_ix", gixL)])
        vc2 = P.emit("v", lambda v, par_e=par_e: (
            v.tensor_copy(gi16[par_e][:, :], tmpf[par_e][:, 0:48]),
            v.scalar_tensor_tensor(tmpf[par_e][:, 48:96],
                                   tmpf[par_e][:, 48:96],
                                   float(TPC + 1), gi_p[par_e][:, 0:96:2],
                                   ALU.mult, ALU.add))[-1],
            waits=[("v", vc1)])
        VC[e] = vc = P.emit("v", lambda v, par_e=par_e: v.tensor_copy(
            si16[par_e][:, :], tmpf[par_e][:, 48:96]),
            waits=[("v", vc2), ("g_sc", SCH.get((e - 2, 1), 0))])
        GA[e] = P.emit("g", lambda g, par_e=par_e: g.dma_gather(
            pool_fm[par_e][:, :].rearrange("p (c n) -> p c n", c=NDC),
            y1d[:, :], gi16[par_e][:, :], CAP, CAP, DM,
            transpose=True, queue_num=1),
            waits=[("v", vc), ("t", POOL_LAST.get(e - 2, 0))] +
                  (y1d_done if e == 0 else []),
            sem=f"g_ga{par_e}")

    FW = {}
    ab_last = {}
    o_done_t = {}
    o_evac_v = {}
    POOL_LAST = {}
    VC = {}
    GA = {}
    HT_WBB = {}
    SCH = {}
    shared_ab = {}
    RDST = {}
    GIX = {}
    WBB = {}
    prevld = {"gix": 0, "wbb": 0}
    sc_add_T = 0
    rd_init = 0
    ffn_s_last, ffn_h1_last = 0, 0
    for ei in range(NE + 1):
        e = ei - 1
        if DBG_SKIP_ROUTED and ei >= DBG_SKIP_ROUTED:
            continue
        s1d = ws1_d[:, :] if e < 0 else we1_d[e, :, :]
        s3d = ws3_d[:, :] if e < 0 else we3_d[e, :, :]
        s2d = ws2_d[:, :] if e < 0 else we2_d[e, :, :]

        def emit_w13(key, sl, wguard, extra=(), s1d=s1d, s3d=s3d):
            par = sl % 2
            prev13 = P.cnt.get("g_we13", 0)
            FW[key + ("w1",)] = P.emit("g", lambda g, sl=sl, par=par,
                                       s1d=s1d: g.dma_start(
                we1s[par][:, :].rearrange("p (c n) -> p c n", c=8),
                s1d[:, sl * 512:(sl + 1) * 512]
                .rearrange("(c p) n -> p c n", p=128)),
                waits=[("t", wguard), ("g_we13", prev13)] + list(extra),
                sem="g_we13")
            FW[key + ("w3",)] = P.emit("g", lambda g, sl=sl, par=par,
                                       s3d=s3d: g.dma_start(
                we3s[par][:, :].rearrange("p (c n) -> p c n", c=8),
                s3d[:, sl * 512:(sl + 1) * 512]
                .rearrange("(c p) n -> p c n", p=128)), sem="g_we13")

        def emit_w2(ei_t, s2d_t, extra=()):
            for hf in range(2):
                prev2 = P.cnt.get("g_we2", 0)
                FW[(ei_t, hf, "w2")] = P.emit("g", lambda g, s2d_t=s2d_t,
                                              hf=hf: g.dma_start(
                    we2h[hf][:, :].rearrange("p (c n) -> p c n", c=16),
                    s2d_t[:, hf * 512:(hf + 1) * 512]
                    .rearrange("(c p) n -> p c n", p=128)),
                    waits=[("t", o_done_t.get(ei_t - 1, ln1_mmlast)),
                           ("g_we2", prev2)] + list(extra), sem="g_we2")
        if ei == 1:
            for sl in range(2):
                emit_w13((ei, sl), sl,
                         ab_last.get((ei - 1, sl + 2),
                                     shared_ab.get((NQT - 1, sl + 2), 0)))
        if ei <= 1:
            emit_w2(ei, s2d, space_guard if ei == 0 else ())

        if ei == 0:
            ht = {}
            sls = {}
            for tt in range(NQT):
                if tt == 1:
                    MD["done"] = emit_md_tail()
                for sl in range(2):
                    emit_w13((0, tt, sl), sl,
                             shared_ab.get((tt - 1, sl + 2), 0),
                             space_guard if tt == 0 else ())
                for fc in range(NFC):
                    if fc in (4, 8):
                        emit_w13((0, tt, fc // 4 + 1), fc // 4 + 1,
                                 shared_ab[(tt, fc // 4 - 1)])
                    par = fc % 2

                    def abf(t_, fc=fc, par=par, tt=tt):
                        for di in range(NDC):
                            t_.matmul(ps_s[par][:, 0:QT],
                                      we1s[(fc // 4) % 2][:, di * 512 +
                                                          (fc % 4) * 128:
                                                          di * 512 +
                                                          (fc % 4 + 1) * 128],
                                      y1[di][:, tt * QT:(tt + 1) * QT],
                                      start=(di == 0), stop=(di == NDC - 1))
                        for di in range(NDC):
                            ii = t_.matmul(ps_s[par][:, QT:2 * QT],
                                           we3s[(fc // 4) % 2][:, di * 512 +
                                                               (fc % 4) * 128:
                                                               di * 512 +
                                                               (fc % 4 + 1) * 128],
                                           y1[di][:, tt * QT:(tt + 1) * QT],
                                           start=(di == 0),
                                           stop=(di == NDC - 1))
                        return ii
                    w = [("g_we13", FW[(0, tt, fc // 4, "w3")]),
                         ("v", ht.get((tt, fc - 2), 0)),
                         ("s", sls.get((tt, fc - 2), 0)),
                         ("v", vsw if tt == 0 and fc < 2 else 0),
                         ("s", st1.get("pss_guard_s", 0)
                          if tt == 0 and fc < 2 else 0)]
                    ab = P.emit("t", abf, waits=w)
                    if fc % 4 == 3:
                        shared_ab[(tt, fc // 4)] = P.cnt["t"]
                    sls[(tt, fc)] = sl_ = P.emit(
                        "s", lambda s, par=par: s.activation(
                            sa_p[par][:, :], ps_s[par][:, 0:QT], AF.Silu),
                        waits=[("t", ab)])
                    ffn_s_last = sl_
                    h1 = P.emit("v", lambda v, par=par: v.tensor_tensor(
                        h1_p[par][:, :], sa_p[par][:, :],
                        ps_s[par][:, QT:2 * QT], ALU.mult),
                        waits=[("s", sl_)])
                    ffn_h1_last = h1
                    hdst_s = h_side[0] if fc < 4 else h_pool
                    fcb_s = fc if fc < 4 else fc - 4
                    ht[(tt, fc)] = P.emit(
                        "v", lambda v, fcb_s=fcb_s, fc=fc, hdst_s=hdst_s:
                        v.tensor_copy(
                            hdst_s[:, fcb_s * CAP:fcb_s * CAP + 512],
                            h1_p[fc % 2][:, :]),
                        waits=[("v", h1),
                               ("t", o_done_t.get((0, tt - 1), 0))])
                for sub in range(4):
                    tok = tt * QT + sub * 128
                    for hf in range(2):
                        def omf(t_, tt=tt, sub=sub, hf=hf):
                            for fc in range(NFC):
                                hsrc = h_side[0] if fc < 4 else h_pool
                                fcb = fc if fc < 4 else fc - 4
                                ii = t_.matmul(
                                    ps_c[sub % 2][:, 0:QT],
                                    hsrc[:, fcb * CAP + sub * 128:
                                         fcb * CAP + (sub + 1) * 128],
                                    we2h[hf][:, fc * 512:(fc + 1) * 512],
                                    start=(fc == 0), stop=(fc == NFC - 1))
                            return ii
                        w = [("v", ht[(tt, NFC - 1)]),
                             ("g_we2", FW[(ei, hf, "w2")]),
                             ("v", o_evac_v.get((tt, sub, 0), 0)
                              if hf == 1 else 0),
                             ("v", vsw if tt == 0 and sub < 2 else 0),
                             ("v", o_evac_v.get((tt, sub - 2),
                                  o_evac_v.get((tt - 1, sub + 2), 0)))]
                        ot_t = P.emit("t", omf, waits=w)
                        o_evac_v[(tt, sub, hf)] = P.emit(
                            "v", lambda v, sub=sub, hf=hf: v.tensor_copy(
                                ost[sub % 2][:, hf * 512:(hf + 1) * 512],
                                ps_c[sub % 2][:, 0:QT]),
                            waits=[("t", ot_t),
                                   (f"y_rd{sub % 2}", RDST.get(sub % 2, 0)
                                    if hf == 0 else 0)])
                    o_evac_v[(tt, sub)] = o_evac_v[(tt, sub, 1)]
                    rd_init = P.emit(
                        "y", lambda y, tok=tok, sub=sub: y.dma_start(
                            rdd[tok:tok + 128, :], ost[sub % 2][:, 0:DM]),
                        waits=[("v", o_evac_v[(tt, sub, 1)])],
                        sem=f"y_rd{sub % 2}")
                    RDST[sub % 2] = rd_init
                o_done_t[(0, tt)] = P.cnt["t"]
            o_done_t[0] = P.cnt["t"]
        else:
            par_e = e % 2
            if e == 0:
                md_done = MD["done"]
                emit_exp_loads(0, md_done)
            ps_guard_s, ps_guard_v = ffn_s_last, ffn_h1_last
            ht = {}
            sls = {}
            si = 0
            for fc in range(NFC):
                if fc in (4, 8) and True:
                    emit_w13((ei, fc // 4 + 1), fc // 4 + 1,
                             ab_last[(ei, fc // 4 - 1)])
                for (lo, hi) in SUBS:
                    n = hi - lo
                    par = si % 2
                    si += 1

                    def abf(t_, fc=fc, lo=lo, hi=hi, n=n, par=par,
                            par_e=par_e):
                        for di in range(NDC):
                            t_.matmul(ps_s[par][:, 0:n],
                                      we1s[(fc // 4) % 2][:, di * 512 +
                                                          (fc % 4) * 128:
                                                          di * 512 +
                                                          (fc % 4 + 1) * 128],
                                      pool_fm[par_e][:, di * CAP + lo:
                                                     di * CAP + hi],
                                      start=(di == 0), stop=(di == NDC - 1))
                        for di in range(NDC):
                            ii = t_.matmul(ps_s[par][:, QT:QT + n],
                                           we3s[(fc // 4) % 2][:, di * 512 +
                                                               (fc % 4) * 128:
                                                               di * 512 +
                                                               (fc % 4 + 1) * 128],
                                           pool_fm[par_e][:, di * CAP + lo:
                                                          di * CAP + hi],
                                           start=(di == 0),
                                           stop=(di == NDC - 1))
                        return ii
                    w = [("g_we13", FW[(ei, fc // 4, "w3")]),
                         ("v", ht.get(si - 2, 0)), ("s", sls.get(si - 2, 0)),
                         (f"g_ga{par_e}", GA[e] if si == 1 else 0),
                         ("s", ps_guard_s if si <= 2 else 0),
                         ("v", ps_guard_v if si <= 2 else 0)]
                    ab = P.emit("t", abf, waits=w)
                    if fc in (3, 7, 11, NFC - 1) and lo != 0:
                        ab_last[(ei, fc // 4)] = P.cnt["t"]
                    sls[si] = sl_ = P.emit(
                        "s", lambda s, par=par, n=n: s.activation(
                            sa_p[par][:, 0:n], ps_s[par][:, 0:n], AF.Silu),
                        waits=[("t", ab)])
                    ffn_s_last = sl_
                    h1 = P.emit("v", lambda v, par=par, n=n: v.tensor_tensor(
                        h1_p[par][:, 0:n], sa_p[par][:, 0:n],
                        ps_s[par][:, QT:QT + n], ALU.mult),
                        waits=[("s", sl_)])
                    ffn_h1_last = h1
                    hdst = h_side[par_e] if fc < 4 else h_pool
                    fcb = fc if fc < 4 else fc - 4
                    ht[si] = P.emit(
                        "v", lambda v, fcb=fcb, lo=lo, hi=hi, par=par,
                        hdst=hdst, par_e=par_e: v.tensor_tensor(
                            hdst[:, fcb * CAP + lo:fcb * CAP + hi],
                            h1_p[par][:, 0:hi - lo],
                            mdw[par_e][:, 2 * lo + 1:2 * hi:2],
                            ALU.mult),
                        waits=[("v", h1),
                               ("g_wb", WBB[e] if si == 1 else 0),
                               ("t", o_done_t.get(ei - 2, 0)
                                if si == 1 else 0),
                               ("t", o_done_t.get(ei - 1, 0)
                                if si == 9 else 0)])
            POOL_LAST[e] = P.cnt["t"]
            h_done = ht[si]
            HT_WBB[e] = P.cnt["v"]
            if e + 1 < NE:
                emit_exp_loads(e + 1, md_done)
                for sl in range(2):
                    emit_w13((ei + 1, sl), sl, ab_last[(ei, sl + 2)],
                             s1d=we1_d[e + 1, :, :], s3d=we3_d[e + 1, :, :])
            for g6 in range(NG):
                hhalf = g6 // 3
                col = (g6 % 3) * DM
                for hf in range(2):
                    def omf(t_, g6=g6, hf=hf, par_e=par_e):
                        for fc in range(NFC):
                            hsrc = h_side[par_e] if fc < 4 else h_pool
                            fcb = fc if fc < 4 else fc - 4
                            ii = t_.matmul(
                                ps_c[g6 % 2][:, 0:QT],
                                hsrc[:, fcb * CAP + g6 * 128:
                                     fcb * CAP + (g6 + 1) * 128],
                                we2h[hf][:, fc * 512:(fc + 1) * 512],
                                start=(fc == 0), stop=(fc == NFC - 1))
                        return ii
                    w = [("v", h_done), ("g_we2", FW[(ei, hf, "w2")]),
                         ("v", o_evac_v.get((ei, g6, 0), 0)
                          if hf == 1 else 0),
                         ("v", o_evac_v.get((ei, g6 - 2),
                              o_evac_v.get((ei - 1, g6 + NG - 2),
                                           o_evac_v.get((NQT - 1, g6 + 2),
                                                        0))))]
                    ot_t = P.emit("t", omf, waits=w)
                    o_evac_v[(ei, g6, hf)] = P.emit(
                        "v", lambda v, g6=g6, hf=hf, col=col, hhalf=hhalf:
                        v.tensor_copy(
                            ost[hhalf][:, col + hf * 512:col + hf * 512 + 512],
                            ps_c[g6 % 2][:, 0:QT]),
                        waits=[("t", ot_t),
                               ("g_sc", SCH.get((e - 1, hhalf), 0)
                                if g6 % 3 == 0 else 0),
                               ("y_rd0", RDST.get(0, 0)
                                if e == 0 and g6 % 3 == 0 else 0),
                               ("y_rd1", RDST.get(1, 0)
                                if e == 0 and g6 % 3 == 0 else 0)])
                o_evac_v[(ei, g6)] = o_evac_v[(ei, g6, 1)]
                if g6 % 3 == 2:
                    sc_add_T = P.emit(
                        "g", lambda g, par_e=par_e, hhalf=hhalf:
                        g.dma_scatter_add(
                            rdd[:, :],
                            ost[hhalf][:, :].rearrange("p (j e2) -> p j e2",
                                                       j=3),
                            si16[par_e][:, 24 * hhalf:24 * hhalf + 24],
                            384, 384, DM, queue_num=2),
                        waits=[("v", o_evac_v[(ei, g6, 1)]),
                               ("g_sc", sc_add_T),
                               ("y_rd0", RDST.get(0, 0) if e == 0 else 0),
                               ("y_rd1", RDST.get(1, 0) if e == 0 else 0)],
                        sem="g_sc")
                    SCH[(e, hhalf)] = sc_add_T
            o_done_t[ei] = P.cnt["t"]
            if e + 1 < NE:
                emit_w2(ei + 1, we2_d[e + 1, :, :])
    adds_done = sc_add_T
    sc_f.close()
    sc_l2 = contextlib.ExitStack()
    NPL = 4
    rt2 = [sc_l2.enter_context(nc.sbuf_tensor(f"rt2{p}", [128, DM], F32))
           for p in range(NPL)]
    y1t2 = [sc_l2.enter_context(nc.sbuf_tensor(f"y1t2{p}", [128, DM], F32))
            for p in range(NPL)]
    zt2 = [sc_l2.enter_context(nc.sbuf_tensor(f"zt2{p}", [128, DM], F32))
           for p in range(NPL)]
    zq2 = [sc_l2.enter_context(nc.sbuf_tensor(f"zq2{p}", [128, DM], F32))
           for p in range(NPL)]
    lrow = sc_l2.enter_context(nc.sbuf_tensor("lrow", [128, 32], F32))
    y2o = [sc_l2.enter_context(nc.sbuf_tensor(f"y2o{p}", [128, DM], F32))
           for p in range(NPL)]
    L2 = {}
    prev_st = {p: 0 for p in range(NPL)}
    LSTR = "abcd"
    for grp in range(4):
        tiles = [4 * grp + k for k in range(NPL)]
        for ti in tiles:
            par = ti % NPL
            tok = ti * 128
            ld1 = P.emit("y", lambda y, par=par, tok=tok: y.dma_start(
                rt2[par][:, :], rdd[tok:tok + 128, :]),
                waits=[("g_sc", adds_done), ("v", L2.get((ti - NPL, "z"), 0)),
                       ("y_rd0", RDST.get(0, 0)), ("y_rd1", RDST.get(1, 0)),
                       ("t", o_done_t.get(NE, o_done_t.get(0, 0))),
                       ("v", ffn_h1_last), ("s", ffn_s_last)] + y1d_done,
                sem=f"y_l2{LSTR[par]}")
            L2[(ti, "ld")] = P.emit(
                "y", lambda y, par=par, tok=tok: y.dma_start(
                    y1t2[par][:, :], y1f[tok:tok + 128, :]),
                sem=f"y_l2{LSTR[par]}")
        for ti in tiles:
            par = ti % NPL
            L2[(ti, "z")] = P.emit(
                "v", lambda v, par=par: v.scalar_tensor_tensor(
                    zt2[par][:, :], rt2[par][:, :], 1.0 / 3.0,
                    y1t2[par][:, :], ALU.mult, ALU.add),
                waits=[(f"y_l2{LSTR[par]}", L2[(ti, "ld")]),
                       ("v", L2.get((ti - NPL, "fin"), 0))])
        for ti in tiles:
            par = ti % NPL
            L2[(ti, "sq")] = P.emit("s", lambda s, par=par: s.activation(
                zq2[par][:, :], zt2[par][:, :], AF.Square),
                waits=[("v", L2[(ti, "z")])])
        for ti in tiles:
            par = ti % NPL
            L2[(ti, "m")] = P.emit("v", lambda v, par=par: v.reduce_sum(
                lrow[:, par * 8:par * 8 + 1], zt2[par][:, :], axis=AX.X),
                waits=[("v", L2[(ti, "z")])])
        for ti in tiles:
            par = ti % NPL
            L2[(ti, "s2")] = P.emit("v", lambda v, par=par: v.reduce_sum(
                lrow[:, par * 8 + 1:par * 8 + 2], zq2[par][:, :], axis=AX.X),
                waits=[("s", L2[(ti, "sq")])])
        for ti in tiles:
            b8 = (ti % NPL) * 8
            L2[(ti, "r2")] = P.emit("v", lambda v, b8=b8: (
                v.tensor_scalar(lrow[:, b8 + 2:b8 + 3],
                                lrow[:, b8:b8 + 1], 1.0 / DM, None, ALU.mult),
                v.tensor_scalar(lrow[:, b8 + 3:b8 + 4],
                                lrow[:, b8 + 1:b8 + 2], 1.0 / DM, None,
                                ALU.mult))[-1],
                waits=[("v", L2[(ti, "s2")])])
        for ti in tiles:
            b8 = (ti % NPL) * 8
            L2[(ti, "r3")] = P.emit("v", lambda v, b8=b8:
                v.scalar_tensor_tensor(
                    lrow[:, b8 + 5:b8 + 6], lrow[:, b8 + 2:b8 + 3],
                    -1.0, lrow[:, b8 + 2:b8 + 3], ALU.mult, ALU.mult),
                waits=[("v", L2[(ti, "r2")])])
        for ti in tiles:
            b8 = (ti % NPL) * 8
            L2[(ti, "r3b")] = P.emit("v", lambda v, b8=b8: v.tensor_tensor(
                lrow[:, b8 + 5:b8 + 6], lrow[:, b8 + 3:b8 + 4],
                lrow[:, b8 + 5:b8 + 6], ALU.add),
                waits=[("v", L2[(ti, "r3")])])
        for ti in tiles:
            b8 = (ti % NPL) * 8
            L2[(ti, "r4")] = P.emit("s", lambda s, b8=b8: s.activation(
                lrow[:, b8 + 6:b8 + 7], lrow[:, b8 + 5:b8 + 6], AF.Sqrt,
                bias=1e-5), waits=[("v", L2[(ti, "r3b")])])
        for ti in tiles:
            b8 = (ti % NPL) * 8
            L2[(ti, "r5")] = P.emit("v", lambda v, b8=b8: v.reciprocal(
                lrow[:, b8 + 7:b8 + 8], lrow[:, b8 + 6:b8 + 7]),
                waits=[("s", L2[(ti, "r4")])])
        for ti in tiles:
            par = ti % NPL
            b8 = par * 8
            L2[(ti, "fin")] = P.emit("v", lambda v, par=par, b8=b8:
                                     v.tensor_scalar(
                y2o[par][:, :], zt2[par][:, :], lrow[:, b8 + 2:b8 + 3],
                lrow[:, b8 + 7:b8 + 8], ALU.subtract, ALU.mult),
                waits=[("v", L2[(ti, "r5")]),
                       (f"y_st{par}", prev_st[par])])
        for ti in tiles:
            par = ti % NPL
            tok = ti * 128
            prev_st[par] = P.emit(
                "y", lambda y, par=par, tok=tok: y.dma_start(
                    out_d[tok:tok + 128, :], y2o[par][:, :]),
                waits=[("v", L2[(ti, "fin")])], sem=f"y_st{par}")
    P.emit("y", lambda y: y.nop(),
           waits=[(f"y_st{p}", prev_st[p]) for p in range(NPL)], inc=False)

    P.finalize(nc, sems)
    sc_l2.close()
    stk.close()
    return nc


def _host_inputs(inputs):
    x = np.ascontiguousarray(np.asarray(inputs["L0_attn_hidden_states"],
                                        np.float32))
    up = np.asarray(inputs["L0_attn_up"], np.float32).reshape(-1)
    uz = np.asarray(inputs["L0_attn_uz"], np.float32).reshape(-1)
    upz = np.zeros((128, 512), np.float32)
    r = np.arange(128)
    for dc in range(8):
        h = 2 * dc + r // 64
        upz[r, dc * 64 + h] = up[dc * 128 + r]
        upz[r, dc * 64 + 32 + h] = uz[dc * 128 + r]
    wg = np.asarray(inputs["L0_ffn_Wg"], np.float32)
    wg_m = np.ascontiguousarray(wg.reshape(8, 128, 8).transpose(1, 0, 2)
                                .reshape(128, 64))

    def colmat(name):
        a = np.asarray(inputs[name], np.float32).reshape(-1)
        return np.ascontiguousarray(a.reshape(8, 128).T)

    def arr(name):
        return np.ascontiguousarray(np.asarray(inputs[name], np.float32))

    ltri = (np.arange(128)[:, None] < np.arange(128)[None, :]
            ).astype(np.float32)
    ecap = np.broadcast_to((np.arange(8) * 768).astype(np.float32),
                           (128, 8)).copy()
    shared = {
        "eye": np.eye(128, dtype=np.float32),
        "ltri": ltri, "ecap": ecap,
        "wq": arr("L0_attn_wq"), "wk": arr("L0_attn_wk"),
        "wv": arr("L0_attn_wv"), "wo": arr("L0_attn_wo"),
        "wfq": arr("L0_attn_wfq"), "wfg": arr("L0_attn_wfg"),
        "bq_m": colmat("L0_attn_bq"), "bk_m": colmat("L0_attn_bk"),
        "bo_m": colmat("L0_attn_bo"), "bfq_m": colmat("L0_attn_bfq"),
        "bfg_m": colmat("L0_attn_bfg"),
        "bv_m": colmat("L0_attn_bv"),
        "bv_r": np.ascontiguousarray(
            np.asarray(inputs["L0_attn_bv"], np.float32).reshape(1, -1)),
        "upz_m": upz, "wg_m": wg_m,
        "ws1": arr("L0_ffn_Ws1"), "ws3": arr("L0_ffn_Ws3"),
        "ws2": arr("L0_ffn_Ws2"), "we1": arr("L0_ffn_We1"),
        "we3": arr("L0_ffn_We3"), "we2": arr("L0_ffn_We2"),
    }
    in_maps = []
    for c in range(8):
        b, off = c // 2, (c % 2) * TPC
        m = dict(shared)
        m["xr"] = np.ascontiguousarray(
            np.concatenate([x[b, off:], x[b, :off]], axis=0))
        m["jv"] = np.ascontiguousarray(
            np.concatenate([np.arange(off, S), np.arange(0, off)])
            .astype(np.float32).reshape(1, S))
        in_maps.append(m)
    return in_maps


_NC_CACHE = {}


def kernel(**inputs):
    if "nc" not in _NC_CACHE:
        nc_ = build_full()
        mybir.codegen_inst_isa_subclasses(nc_)
        _NC_CACHE["nc"] = nc_
    nc = _NC_CACHE["nc"]
    in_maps = _host_inputs(inputs)
    res = run_bass_kernel_spmd(nc, in_maps, core_ids=list(range(8)))
    out = np.empty((B, S, DM), np.float32)
    for c in range(8):
        b, off = c // 2, (c % 2) * TPC
        out[b, off:off + TPC] = res.results[c]["out"]
    return out
